# version 03 (final)
# baseline (speedup 1.0000x reference)
import math
import numpy as np
from contextlib import ExitStack

import concourse.bass as bass
import concourse.mybir as mybir
from concourse.bass_utils import run_bass_kernel_spmd
import ml_dtypes

F32 = mybir.dt.float32
BF16 = mybir.dt.bfloat16
AF = mybir.ActivationFunctionType
ALU = mybir.AluOpType
AX = mybir.AxisListType

BF = ml_dtypes.bfloat16

N_CORES = 8
B, S, D = 16, 2048, 1024
BPC = B // N_CORES
T = BPC * S
NCD = D // 128
H, HD = 16, 64
N_BLOCKS = 4
DEBUG_L1 = False

ENGINES = ("tensor", "scalar", "vector", "gpsimd", "sync")


class P:
    def __init__(self, nc, stack):
        self.nc = nc
        self.stack = stack
        self.ops = []
        self.sems = {}
        self.cnt = {}
        self.waited = {e: {} for e in ENGINES}
        for name in ("t", "s", "v", "p", "c"):
            self.sems[name] = stack.enter_context(nc.semaphore("sem_" + name))
            self.cnt[name] = 0
        self.stream_last = {}

    def ssem(self, key):
        if key not in self.sems:
            self.sems[key] = self.stack.enter_context(self.nc.semaphore("sem_" + key))
            self.cnt[key] = 0
        return self.sems[key]

    def dma(self, engine, stream, fn):
        self.ssem(stream)
        self.wait(engine, self.stream_last.get(stream))
        tok = self.emit(engine, fn, stream, 16)
        self.stream_last[stream] = tok
        return tok

    def emit(self, engine, fn, sem=None, n=1):
        if sem is None:
            self.ops.append((engine, fn))
            return None
        self.cnt[sem] += n
        val = self.cnt[sem]
        h = self.sems[sem]

        def fn2(eng, fn=fn, h=h, n=n):
            fn(eng).then_inc(h, n)

        self.ops.append((engine, fn2))
        return (sem, val)

    def wait(self, engine, tok):
        if tok is None:
            return
        sem, val = tok
        if val <= self.waited[engine].get(sem, 0):
            return
        self.waited[engine][sem] = val
        h = self.sems[sem]
        self.ops.append((engine, lambda eng, h=h, val=val: eng.wait_ge(h, val)))

    def waits(self, engine, toks):
        for tk in toks:
            self.wait(engine, tk)

    def build(self):
        with self.nc.Block() as block:
            def mk(ename):
                def body(eng):
                    for e, fn in self.ops:
                        if e == ename:
                            fn(eng)
                return body
            block.tensor(mk("tensor"))
            block.scalar(mk("scalar"))
            block.vector(mk("vector"))
            block.gpsimd(mk("gpsimd"))
            block.sync(mk("sync"))


class PsumPool:
    def __init__(self, p, tiles):
        self.p = p
        self.tiles = tiles
        self.free = list(range(len(tiles)))
        self.tok = [None] * len(tiles)

    def acquire(self):
        assert self.free, "psum pool exhausted"
        i = self.free.pop(0)
        return self.tiles[i], i, self.tok[i]

    def release(self, i, tok):
        self.tok[i] = tok
        self.free.append(i)


def mkap(handle, offset, dims):
    free_size = 1
    for d in handle.shape[1:]:
        free_size *= d
    return bass.AP(tensor=handle, offset=offset, ap=[[free_size, 128]] + dims)



def host_constants():
    c = {}
    c["ident_bf"] = np.eye(128, dtype=BF)
    c["ident_f32"] = np.eye(128, dtype=np.float32)
    c["ones_col_bf"] = np.ones((128, 1), dtype=BF)
    c["ones_row_f32"] = np.ones((1, 128), dtype=np.float32)
    inv_freq = 1.0 / (10000.0 ** (np.arange(0, HD, 2, dtype=np.float64) / HD))
    tpos = np.arange(S, dtype=np.float64)
    freqs = np.outer(tpos, inv_freq)
    emb = np.concatenate([freqs, freqs], axis=-1)
    cos_tm = np.cos(emb).reshape(16, 128, HD).transpose(1, 0, 2)
    sin_tm = np.sin(emb).reshape(16, 128, HD).transpose(1, 0, 2)
    sgn_tm = np.concatenate([-np.ones((1, 1, 32)), np.ones((1, 1, 32))], axis=2)
    c["rope_cos"] = cos_tm.reshape(128, 16 * HD).astype(BF)
    c["rope_sinp"] = (sin_tm * sgn_tm).reshape(128, 16 * HD).astype(BF)
    kk = np.arange(128)[:, None]
    qq = np.arange(512)[None, :]
    c["mask_causal"] = np.concatenate(
        [((128 * j + kk) <= qq).astype(BF) for j in range(4)], axis=1)
    wm = []
    for j in range(8):
        kabs = 128 * (j - 4) + kk
        keep = (kabs <= qq) & (kabs >= qq - 512)
        wm.append(np.where(keep, 1e9, -1e9).astype(BF))
    c["mask_window"] = np.concatenate(wm, axis=1)
    krel = (np.arange(1024) - 512).astype(np.float64)
    k_hi = 8.0 * np.floor(krel / 8.0)
    k_lo = krel - k_hi
    augk = np.zeros((128, 1024), dtype=BF)
    for base in (0, 64):
        augk[base + 0] = k_hi.astype(BF)
        augk[base + 1] = k_lo.astype(BF)
        augk[base + 2] = np.ones(1024, dtype=BF)
    c["l1_augk"] = augk
    akh = np.zeros((16, 3, 1024), dtype=BF)
    for h in range(16):
        s8 = 8.0 * 2.0 ** (-0.5 * (h + 1))
        v = s8 * krel
        vhi = v.astype(BF).astype(np.float64)
        vlo = (v - vhi).astype(BF)
        akh[h, 0] = vhi.astype(BF)
        akh[h, 1] = vlo
        akh[h, 2] = np.full(1024, s8, dtype=BF)
    c["l1_augkh"] = akh
    augqc = np.zeros((128, 512), dtype=BF)
    for base in (0, 64):
        augqc[base + 0] = 1.0
        augqc[base + 1] = 1.0
        augqc[base + 2] = (-np.arange(512, dtype=np.float64)).astype(BF)
    c["l1_augqc"] = augqc
    oh_sel = np.zeros((8, 8, 128), dtype=BF)
    for e in range(8):
        oh_sel[e, e, :] = 1.0
    c["onehot8"] = oh_sel.transpose(1, 0, 2).reshape(8, 8 * 128)
    e16 = np.zeros((128, 8, 16), dtype=BF)
    for cp in range(8):
        e16[0:64, cp, 2 * cp] = 1.0
        e16[64:128, cp, 2 * cp + 1] = 1.0
    c["e16sel"] = e16.reshape(128, 128)
    selh = np.zeros((16, 8, 128), dtype=BF)
    for cp in range(8):
        selh[2 * cp, cp, 0:64] = 1.0
        selh[2 * cp + 1, cp, 64:128] = 1.0
    c["selh"] = selh.transpose(0, 1, 2).reshape(16, 1024)
    idx = np.arange(D, dtype=np.float64)
    ang = -2.0 * np.pi * np.outer(idx, idx) / D
    Fr, Fi = np.cos(ang), np.sin(ang)
    Gr, Gi = Fr.T / D, -Fi.T / D
    c["dft_fr"] = Fr.astype(BF)
    c["dft_fi"] = Fi.astype(BF)
    c["dft_fineg"] = (-Fi).astype(BF)
    c["dft_gr"] = Gr.astype(BF)
    c["dft_gineg"] = (-Gi).astype(BF)
    return c



class Build:
    def __init__(self, const_shapes, weight_shapes):
        nc = bass.Bass("TRN2", target_bir_lowering=False)
        self.nc = nc
        self.stack = ExitStack()
        self.p = P(nc, self.stack)
        dr = {}
        dr["x"] = nc.dram_tensor("x", [BPC, S, D], F32, kind="ExternalInput").ap()
        dr["out"] = nc.dram_tensor("out", [BPC, S, D], F32, kind="ExternalOutput").ap()
        for name, (shape, dt) in const_shapes.items():
            dr[name] = nc.dram_tensor(name, list(shape), dt, kind="ExternalInput").ap()
        for name, shape in weight_shapes.items():
            dr[name] = nc.dram_tensor(name, list(shape), F32, kind="ExternalInput").ap()
        self.dr = dr
        self.scr = {}

        A = nc.alloc_sbuf_tensor
        sb = {}
        sb["X"] = A("X", [128, NCD, T], BF16)
        sb["IDB"] = A("IDB", [128, 128], BF16)
        sb["IDF"] = A("IDF", [128, 128], F32)
        sb["STG"] = A("STG", [128, 1024], BF16)
        sb["AR1"] = A("AR1", [128, NCD, 2048], BF16)
        sb["AR2"] = A("AR2", [128, 16, 1040], BF16)
        sb["AR3"] = A("AR3", [128, 2, NCD, 1024], BF16)
        sb["AR5"] = A("AR5", [128, 2, NCD, 512], BF16)
        sb["AR6"] = A("AR6", [128, 5120], BF16)
        sb["OH8"] = A("OH8", [8, 1024], BF16)
        sb["SCR9"] = A("SCR9", [128, NCD, 512], BF16)
        sb["ROW"] = A("ROW", [128, 2048], F32)
        sb["ONEB"] = A("ONEB", [128, 1], BF16)
        sb["ONER"] = A("ONER", [1, 128], F32)
        sb["CONSTF"] = A("CONSTF", [128, 16], F32)
        self.sb = sb
        self.psum = [nc.alloc_psum_tensor(f"pb{i}", [128, 512], F32) for i in range(8)]
        self.pp = PsumPool(self.p, self.psum)

        self.XNS = sb["SCR9"].ap()
        f9 = sb["SCR9"].ap().rearrange("p c t -> p (c t)")
        self.f9 = f9
        self.PTb = [f9[:, 0:512], f9[:, 512:1024], f9[:, 1024:1536], f9[:, 1536:2048]]
        self.PT23 = f9[:, 1024:2048]
        self.BCS = f9[:, 3072:4096].bitcast(F32)
        f5 = sb["AR5"].ap().rearrange("p a c t -> p (a c t)")
        self.f5 = f5
        self.QT = sb["AR5"][:, 0]
        self.AOT = sb["AR5"][:, 1]
        self.TMP_A = f5[:, 4096:5120].bitcast(F32)
        self.TMP_B = f5[:, 5120:6144].bitcast(F32)
        self.QTM = f5[:, 6144:6656]
        self.SQS = f5[:, 0:4096]
        self.MV = f5
        self.OUTS = f5.bitcast(F32)

        self.x_toks = []
        self.const_tok = {}
        self.wslot_busy = [None, None]
        self.bcrow_busy = {}
        self._rope_prev = None
        self._constf_n = 0
        self._constf_vals = {}
        self.constf_tok = None
        self.vt_consumers = []
        self.xn_toks = []

    def register_float(self, val):
        if val in self._constf_vals:
            return
        j = self._constf_n
        assert j < 16
        self._constf_n += 1
        self._constf_vals[val] = j
        ap = self.sb["CONSTF"][:, j:j + 1]
        self.constf_tok = self.p.emit("vector", lambda e, ap=ap, val=val: e.memset(ap, val), "v", 1)
        self.nc.const_aps.register(F32, val, ap)

    def load_consts(self):
        p, sb, dr = self.p, self.sb, self.dr
        for key, dst in (("ident_bf", "IDB"), ("ident_f32", "IDF"),
                         ("ones_col_bf", "ONEB"), ("ones_row_f32", "ONER")):
            self.const_tok[dst] = p.dma(
                "sync", "cst", lambda e, key=key, dst=dst: e.dma_start(out=sb[dst][:, :], in_=dr[key][:, :]))

    def load_x(self):
        p, sb, pp = self.p, self.sb, self.pp
        x_flat = self.dr["x"].rearrange("b s d -> (b s) d")
        stg_tok = [None, None]
        toks = []
        xld = [self.f9[:, 0:1024], self.f9[:, 1024:2048]]
        for ti in range(T // 128):
            buf = ti % 2
            stg = xld[buf]
            p.wait("gpsimd", stg_tok[buf])
            t_ld = p.dma("gpsimd", f"x{buf}", lambda e, stg=stg, ti=ti: e.dma_start(
                out=stg, in_=x_flat[128 * ti:128 * (ti + 1), :]))
            last_tp = None
            for c in range(NCD):
                tile, pi, free = pp.acquire()
                tb = tile.ap().bitcast(BF16)
                p.wait("tensor", t_ld)
                p.wait("tensor", self.const_tok["IDB"])
                p.wait("tensor", free)
                t_tp = p.emit("tensor", lambda e, tb=tb, stg=stg, c=c: e.transpose(
                    tb[:, 0:128], stg[:, 128 * c:128 * (c + 1)], sb["IDB"][:, :]), "t", 1)
                p.wait("vector", t_tp)
                t_ev = p.emit("vector", lambda e, tb=tb, c=c, ti=ti: e.tensor_copy(
                    sb["X"][:, c, 128 * ti:128 * (ti + 1)], tb[:, 0:128]), "v", 1)
                pp.release(pi, t_ev)
                last_tp = t_tp
                toks.append(t_ev)
            stg_tok[buf] = last_tp
        self.x_toks = toks

    def convert(self, name, transpose):
        p, nc, dr = self.p, self.nc, self.dr
        src = dr[name]
        shape = list(src.shape)
        if not transpose:
            scrA = nc.dram_tensor("scr_" + name, shape, BF16).ap()
            tok = p.dma("gpsimd", "wc_" + name, lambda e, scrA=scrA, src=src, shape=tuple(shape): e.dma_start(
                out=scrA.rearrange(_flat_spec(shape)), in_=src.rearrange(_flat_spec(shape))))
            self.scr[name] = (scrA, tok)
            return
        rows, cols = shape
        scrA = nc.dram_tensor("scrA_" + name, [rows, cols], BF16).ap()
        scrT = nc.dram_tensor("scrT_" + name, [cols, rows], BF16).ap()
        t_cast = p.dma("gpsimd", "wc_" + name, lambda e, scrA=scrA, src=src: e.dma_start(
            out=scrA.rearrange("a b -> (a b)"), in_=src.rearrange("a b -> (a b)")))
        stg = self.sb["STG"]
        tok = None
        piece = 1024
        for b0 in range(cols // 128):
            for r0 in range(0, rows, piece):
                p.wait("sync", t_cast)
                t_l = p.dma("sync", "rt", lambda e, b0=b0, r0=r0, scrA=scrA, stg=stg: e.dma_start_transpose(
                    stg[:, 0:piece], scrA[r0:r0 + piece, 128 * b0:128 * (b0 + 1)]))
                tok = p.dma("sync", "rt", lambda e, b0=b0, r0=r0, scrT=scrT, stg=stg: e.dma_start(
                    out=scrT[128 * b0:128 * (b0 + 1), r0:r0 + piece], in_=stg[:, 0:piece]))
        self.scr[name] = (scrT, tok)

    def wslot(self, slot, name):
        p, sb = self.p, self.sb
        scr, rtok = self.scr[name]
        view = sb["AR3"][:, slot]
        p.wait("sync", rtok)
        p.wait("sync", ("t", p.cnt["t"]))
        tok = p.dma("sync", f"ws{slot}", lambda e, scr=scr, view=view: e.dma_start(
            out=view, in_=scr.rearrange("(c p) o -> p c o", p=128)))
        return view, tok

    def ln_xn(self, xn_scr, eps=1e-5):
        p, sb, pp = self.p, self.sb, self.pp
        R0 = sb["ROW"]
        SQS = self.SQS
        XN = sb["AR1"]
        self.xn_toks = []
        prev_t2 = None
        for s in range(BPC):
            p.waits("vector", self.xn_toks)
            ev = []
            for g in range(4):
                cols = slice(2048 * s + 512 * g, 2048 * s + 512 * (g + 1))
                bankA, ia, freeA = pp.acquire()
                bankB, ib, freeB = pp.acquire()
                p.wait("tensor", freeA)
                p.wait("tensor", self.const_tok["ONEB"])
                p.waits("tensor", self.x_toks)
                tokA = None
                for c in range(NCD):
                    tokA = p.emit("tensor", lambda e, bankA=bankA, c=c, cols=cols: e.matmul(
                        bankA.ap()[0:1, :], sb["ONEB"][:, :], sb["X"][:, c, cols],
                        start=(c == 0), stop=(c == NCD - 1)), "t", 1)
                p.waits("scalar", self.x_toks)
                p.wait("scalar", ("t", p.cnt["t"]))
                sq_toks = []
                for c in range(NCD):
                    sq_toks.append(p.emit("scalar", lambda e, c=c, cols=cols, SQS=SQS: e.activation(
                        SQS[:, 512 * c:512 * (c + 1)], sb["X"][:, c, cols], AF.Square), "s", 1))
                p.wait("tensor", freeB)
                p.waits("tensor", sq_toks)
                tokB = None
                for c in range(NCD):
                    tokB = p.emit("tensor", lambda e, bankB=bankB, c=c, SQS=SQS: e.matmul(
                        bankB.ap()[0:1, :], sb["ONEB"][:, :], SQS[:, 512 * c:512 * (c + 1)],
                        start=(c == 0), stop=(c == NCD - 1)), "t", 1)
                p.wait("vector", tokA)
                t_mu = p.emit("vector", lambda e, bankA=bankA: e.tensor_scalar(
                    R0[0:1, 0:512], bankA.ap()[0:1, :], 1.0 / D, None, ALU.mult), "v", 1)
                p.wait("vector", tokB)
                t_m2 = p.emit("vector", lambda e, bankB=bankB: e.tensor_scalar(
                    R0[0:1, 1024:1536], bankB.ap()[0:1, :], 1.0 / D, None, ALU.mult), "v", 1)
                pp.release(ia, t_mu)
                pp.release(ib, t_m2)
                p.wait("vector", t_mu)
                p.wait("vector", prev_t2)
                t_mu2 = p.emit("vector", lambda e: e.tensor_tensor(
                    self.TMP_A[0:1, :], R0[0:1, 0:512], R0[0:1, 0:512], ALU.mult), "v", 1)
                p.wait("vector", t_m2)
                p.wait("vector", t_mu2)
                t_var = p.emit("vector", lambda e: e.tensor_tensor(
                    R0[0:1, 1024:1536], R0[0:1, 1024:1536], self.TMP_A[0:1, :], ALU.subtract), "v", 1)
                p.wait("scalar", t_var)
                p.wait("scalar", self.constf_tok)
                t_sd = p.emit("scalar", lambda e, eps=eps: e.activation(
                    self.TMP_A[0:1, :], R0[0:1, 1024:1536], AF.Sqrt, bias=eps), "s", 1)
                p.wait("vector", t_sd)
                t_r = p.emit("vector", lambda e: e.reciprocal(
                    R0[0:1, 512:1024], self.TMP_A[0:1, :]), "v", 1)
                p.wait("tensor", t_mu)
                bmu, imu, tmu = self.bcast_row(R0[0:1, 0:512])
                p.wait("tensor", t_r)
                brq, irq, trq = self.bcast_row(R0[0:1, 512:1024])
                t2 = None
                p.wait("vector", t_r)
                for c in range(NCD):
                    p.wait("vector", tmu)
                    p.wait("vector", t2)
                    t1 = p.emit("vector", lambda e, c=c, cols=cols, bmu=bmu: e.tensor_tensor(
                        self.TMP_A, sb["X"][:, c, cols], bmu.ap()[:, :], ALU.subtract), "v", 1)
                    p.wait("vector", trq)
                    p.wait("vector", t1)
                    t2 = p.emit("vector", lambda e, c=c, g=g, brq=brq, XN=XN: e.tensor_tensor(
                        XN[:, c, 512 * g:512 * (g + 1)], self.TMP_A, brq.ap()[:, :], ALU.mult), "v", 1)
                    ev.append(t2)
                pp.release(imu, t2)
                pp.release(irq, t2)
                prev_t2 = t2
            for c in range(NCD):
                p.waits("sync", ev)
                t_w = p.dma("sync", "xnw", lambda e, c=c, s=s, XN=XN, xn_scr=xn_scr: e.dma_start(
                    out=xn_scr[c, :, 2048 * s:2048 * (s + 1)], in_=XN[:, c, :]))
                self.xn_toks.append(t_w)

    def bcast_row(self, row_ap, n=128, ncols=512):
        p, pp = self.p, self.pp
        bank, i, free = pp.acquire()
        p.wait("tensor", free)
        p.wait("tensor", self.const_tok["ONER"])
        tok = p.emit("tensor", lambda e, bank=bank, row_ap=row_ap, n=n, ncols=ncols: e.matmul(
            bank.ap()[0:n, 0:ncols], self.sb["ONER"][:, 0:n], row_ap, start=True, stop=True), "t", 1)
        return bank, i, tok

    def load_xns(self, xn_scr, cols, prev_use_toks, src_toks=None):
        p = self.p
        p.waits("sync", self.xn_toks if src_toks is None else src_toks)
        p.waits("sync", prev_use_toks)
        p.wait("sync", ("t", p.cnt["t"]))
        p.wait("sync", ("s", p.cnt["s"]))
        p.wait("sync", ("v", p.cnt["v"]))
        p.wait("sync", self.p.stream_last.get("akh"))
        p.wait("sync", self.p.stream_last.get("kst"))
        return p.dma("sync", "xns", lambda e, cols=cols, xn_scr=xn_scr: e.dma_start(
            out=self.XNS, in_=xn_scr[:, :, cols].rearrange("c p t -> p c t")))


def _flat_spec(shape):
    letters = "abcd"[:len(shape)]
    return " ".join(letters) + " -> (" + " ".join(letters) + ")"



def produce_vt(b, s, xn_scr, WvT, tk_wv, vt_consumers, xns_use_io):
    p, sb, pp = b.p, b.sb, b.pp
    ones_toks = []
    p.waits("vector", vt_consumers)
    for kt in range(16):
        ones_toks.append(p.emit("vector", lambda e, kt=kt: e.memset(
            mkap(sb["AR2"], 1040 * kt + 64, [[65, 16]]), 1.0), "v", 1))
    vt_ev = []
    for g in range(4):
        cols = slice(2048 * s + 512 * g, 2048 * s + 512 * (g + 1))
        xns_tok = b.load_xns(xn_scr, cols, xns_use_io[0])
        xns_use_io[0] = []
        for sub in range(4):
            kt = 4 * g + sub
            for nh in range(2):
                bank, ib, free = pp.acquire()
                p.wait("tensor", free)
                p.wait("tensor", tk_wv)
                p.wait("tensor", xns_tok)
                t_mm = None
                for c in range(NCD):
                    t_mm = p.emit("tensor", lambda e, bank=bank, c=c, nh=nh, sub=sub, WvT=WvT: e.matmul(
                        bank.ap()[:, :], sb["SCR9"][:, c, 128 * sub:128 * (sub + 1)],
                        WvT[:, c, 512 * nh:512 * (nh + 1)],
                        start=(c == 0), stop=(c == NCD - 1)), "t", 1)
                xns_use_io[0].append(t_mm)
                p.wait("vector", t_mm)
                p.waits("vector", ones_toks)
                t_ev = p.emit("vector", lambda e, bank=bank, kt=kt, nh=nh: e.tensor_copy(
                    mkap(sb["AR2"], 1040 * kt + 65 * 8 * nh, [[65, 8], [1, 64]]),
                    bank.ap().rearrange("p (a b) -> p a b", a=8)), "v", 1)
                vt_ev.append(t_ev)
                pp.release(ib, t_ev)
        yield g, xns_tok, vt_ev


def head_epilogue(b, t_pv, pv_bank, ipv, p0, hc, aot_toks):
    p, sb, pp = b.p, b.sb, b.pp
    p.wait("vector", t_pv)
    p.waits("vector", [b.bcrow_busy.get(2)])
    t_r = p.emit("vector", lambda e, pv_bank=pv_bank: e.reciprocal(
        sb["ROW"][0:1, 1024:1536], pv_bank.ap()[64:65, :]), "v", 1)
    p.wait("tensor", t_r)
    bcb, ibc, freebc = pp.acquire()
    p.wait("tensor", freebc)
    t_bc = p.emit("tensor", lambda e, bcb=bcb: e.matmul(
        bcb.ap()[0:64, :], sb["ONER"][:, 0:64], sb["ROW"][0:1, 1024:1536],
        start=True, stop=True), "t", 1)
    b.bcrow_busy[2] = t_bc
    p.wait("vector", t_bc)
    t_bce = p.emit("vector", lambda e, bcb=bcb: e.tensor_copy(
        b.BCS[0:64, :], bcb.ap()[0:64, :]), "v", 1)
    pp.release(ibc, t_bce)
    p.wait("vector", t_bce)
    t_ao = p.emit("vector", lambda e, pv_bank=pv_bank, p0=p0, hc=hc: e.tensor_tensor(
        b.AOT[p0:p0 + 64, hc, :], pv_bank.ap()[0:64, :], b.BCS[0:64, :], ALU.mult), "v", 1)
    aot_toks.append(t_ao)
    pp.release(ipv, t_ao)


def wo_project(b, WoT, tk_wo, aot_toks, cols, new_x_toks):
    p, sb, pp = b.p, b.sb, b.pp
    for oc in range(NCD):
        bank, ibk, free = pp.acquire()
        p.wait("tensor", free)
        p.wait("tensor", tk_wo)
        p.waits("tensor", aot_toks)
        t_mm = None
        for c in range(NCD):
            t_mm = p.emit("tensor", lambda e, bank=bank, c=c, oc=oc, WoT=WoT: e.matmul(
                bank.ap()[:, :], WoT[:, c, 128 * oc:128 * (oc + 1)],
                b.AOT[:, c, :], start=(c == 0), stop=(c == NCD - 1)), "t", 1)
        p.wait("vector", t_mm)
        p.waits("vector", b.x_toks)
        t_x = p.emit("vector", lambda e, bank=bank, oc=oc, cols=cols: e.tensor_tensor(
            sb["X"][:, oc, cols], bank.ap()[:, :], sb["X"][:, oc, cols], ALU.add), "v", 1)
        new_x_toks.append(t_x)
        pp.release(ibk, t_x)



def block0(b: Build):
    p, sb, pp, nc = b.p, b.sb, b.pp, b.nc
    xn_scr = nc.dram_tensor("xn_scr0", [NCD, 128, T], BF16).ap()


    AR6 = sb["AR6"]
    ar6h = sb["AR6"]
    tk_cos = p.dma("sync", "bc", lambda e: e.dma_start(out=AR6[:, 0:1024], in_=b.dr["rope_cos"][:, :]))
    tk_sin = p.dma("sync", "bc", lambda e: e.dma_start(out=AR6[:, 1024:2048], in_=b.dr["rope_sinp"][:, :]))
    tk_msk = p.dma("sync", "bc", lambda e: e.dma_start(out=AR6[:, 2048:4096], in_=b.dr["mask_causal"][:, :]))

    def cos_bc(sub):
        return mkap(ar6h, 64 * sub, [[0, 8], [1, 64]])

    def sinp_bc(sub):
        return mkap(ar6h, 1024 + 64 * sub, [[0, 8], [1, 64]])

    def perm_ap(bank):
        return bass.AP(tensor=bank, offset=32, ap=[[512, 128], [64, 8], [-32, 2], [1, 32]])

    def mask_at(j):
        return AR6[:, 2048 + 512 * j:2048 + 512 * (j + 1)]

    b.ln_xn(xn_scr)

    BIA = sb["ROW"][:, 1536:1544]
    tk_bias = p.dma("gpsimd", "sm", lambda e: e.dma_start(
        out=BIA, in_=b.dr["L0_ffn_bias"].rearrange("(c p) -> p c", p=128),
        allow_slow_non_contiguous=True))
    b._bias0_tok = tk_bias

    KT = sb["AR1"]
    QT, AOT = b.QT, b.AOT
    PTb = b.PTb
    vt_consumers = list(b.vt_consumers)
    kt_consumers = []
    new_x_toks = []
    xns_use_io = [[]]

    def rope_produce(dst_fn, WT, tk_w, xns_tok, sub, tsub, dst_toks, extra_waits):
        for oh in range(2):
            bank, ia, freeA = pp.acquire()
            p.wait("tensor", freeA)
            p.wait("tensor", tk_w)
            p.wait("tensor", xns_tok)
            tA = None
            for c in range(NCD):
                tA = p.emit("tensor", lambda e, bank=bank, c=c, oh=oh, sub=sub, WT=WT: e.matmul(
                    bank.ap()[:, :], sb["SCR9"][:, c, 128 * sub:128 * (sub + 1)],
                    WT[:, c, 512 * oh:512 * (oh + 1)],
                    start=(c == 0), stop=(c == NCD - 1)), "t", 1)
            xns_use_io[0].append(tA)
            p.wait("vector", tA)
            p.wait("vector", tk_cos)
            p.wait("vector", tk_sin)
            p.wait("vector", b._rope_prev)
            t1 = p.emit("vector", lambda e, bank=bank, tsub=tsub: e.tensor_tensor(
                b.TMP_A, bank.ap().rearrange("p (h d) -> p h d", d=64), cos_bc(tsub), ALU.mult), "v", 1)
            t2 = p.emit("vector", lambda e, bank=bank, tsub=tsub: e.tensor_tensor(
                b.TMP_B, perm_ap(bank), sinp_bc(tsub), ALU.mult), "v", 1)
            p.wait("vector", t1)
            p.wait("vector", t2)
            t3 = p.emit("vector", lambda e: e.tensor_tensor(
                b.QTM, b.TMP_A, b.TMP_B, ALU.add), "v", 1)
            b._rope_prev = t3
            pp.release(ia, t3)
            for j in range(4):
                tile, pi, free = pp.acquire()
                tb = tile.ap().bitcast(BF16)
                p.wait("tensor", free)
                p.wait("tensor", t3)
                t_tp = p.emit("tensor", lambda e, tb=tb, j=j: e.transpose(
                    tb[:, 0:128], b.QTM[:, 128 * j:128 * (j + 1)], sb["IDB"][:, :]), "t", 1)
                p.wait("vector", t_tp)
                p.waits("vector", extra_waits)
                t_ev = p.emit("vector", lambda e, tb=tb, dst=dst_fn(j, oh): e.tensor_copy(
                    dst, tb[:, 0:128]), "v", 1)
                dst_toks.append(t_ev)
                pp.release(pi, t_ev)

    for s in range(BPC):
        p.wait("vector", ("t", p.cnt["t"]))
        WvT, tk_wv = b.wslot(0, "L0_attn_Wv")
        kt_ev = []
        vt_ev = []
        for g, xns_tok, vt_ev in produce_vt(b, s, xn_scr, WvT, tk_wv, vt_consumers, xns_use_io):
            if g == 0:
                b._wkt, b._tkwk = b.wslot(1, "L0_attn_Wk")
            WkT, tk_wk = b._wkt, b._tkwk
            for sub in range(4):
                tsub = 4 * g + sub

                def kdst(j, oh, tsub=tsub):
                    return KT[:, 4 * oh + j, 128 * tsub:128 * (tsub + 1)]
                rope_produce(kdst, WkT, tk_wk, xns_tok, sub, tsub, kt_ev,
                             kt_consumers + b.xn_toks)
        vt_consumers = []
        kt_consumers = []

        WqT, tk_wq = b.wslot(0, "L0_attn_Wq")
        WoT, tk_wo = None, None
        for qt in range(4):
            cols = slice(2048 * s + 512 * qt, 2048 * s + 512 * (qt + 1))
            xns_tok = b.load_xns(xn_scr, cols, xns_use_io[0])
            xns_use_io[0] = []
            qt_toks = []
            for sub in range(4):
                tsub = 4 * qt + sub

                def qdst(j, oh, sub=sub):
                    return QT[:, 4 * oh + j, 128 * sub:128 * (sub + 1)]
                rope_produce(qdst, WqT, tk_wq, xns_tok, sub, tsub, qt_toks, [])
            aot_toks = []
            n_kc = 4 * qt + 4
            pt_free = [None, None, None, None]
            for h in range(H):
                p0 = 64 * (h % 2)
                hc = h // 2
                pv_bank, ipv, freepv = pp.acquire()
                p.wait("tensor", freepv)
                sc_banks = [pp.acquire(), pp.acquire()]
                t_pv = None
                for kc in range(n_kc):
                    bank, ibk, freebk = sc_banks[kc % 2]
                    p.wait("tensor", freebk)
                    p.waits("tensor", qt_toks)
                    p.waits("tensor", kt_ev)
                    t_sc = p.emit("tensor", lambda e, bank=bank, p0=p0, hc=hc, kc=kc: e.matmul(
                        bank.ap()[:, :], KT[p0:p0 + 64, hc, 128 * kc:128 * (kc + 1)],
                        QT[p0:p0 + 64, hc, :], start=True, stop=True), "t", 1)
                    ptslot = kc % 4
                    p.wait("scalar", t_sc)
                    p.wait("scalar", pt_free[ptslot])
                    t_exp = p.emit("scalar", lambda e, bank=bank, ptslot=ptslot: e.activation(
                        PTb[ptslot], bank.ap()[:, :], AF.Exp, scale=0.125), "s", 1)
                    sc_banks[kc % 2] = (bank, ibk, t_exp)
                    t_p = t_exp
                    if kc >= 4 * qt:
                        j = kc - 4 * qt
                        p.wait("vector", t_exp)
                        p.wait("vector", tk_msk)
                        t_p = p.emit("vector", lambda e, ptslot=ptslot, j=j: e.tensor_tensor(
                            PTb[ptslot], PTb[ptslot], mask_at(j), ALU.mult), "v", 1)
                    p.wait("tensor", t_p)
                    p.waits("tensor", vt_ev)
                    t_pv = p.emit("tensor", lambda e, pv_bank=pv_bank, kc=kc, h=h, ptslot=ptslot, n_kc=n_kc: e.matmul(
                        pv_bank.ap()[0:65, :], mkap(sb["AR2"], 1040 * kc + 65 * h, [[1, 65]]),
                        PTb[ptslot], start=(kc == 0), stop=(kc == n_kc - 1)), "t", 1)
                    vt_consumers.append(t_pv)
                    pt_free[ptslot] = t_pv
                for bank, ibk, tokk in sc_banks:
                    pp.release(ibk, tokk)
                head_epilogue(b, t_pv, pv_bank, ipv, p0, hc, aot_toks)
            if WoT is None:
                WoT, tk_wo = b.wslot(1, "L0_attn_Wo")
            wo_project(b, WoT, tk_wo, aot_toks, cols, new_x_toks)
    b.vt_consumers = vt_consumers
    b.x_toks = b.x_toks + new_x_toks

    l0_ffn(b, xn_scr)


def l0_ffn(b: Build, xn_scr):
    p, sb, pp, nc = b.p, b.sb, b.pp, b.nc
    wrT, tk_wr = b.wslot(0, "L0_ffn_w_freq_real")
    wiT, tk_wi = b.wslot(1, "L0_ffn_w_freq_imag")
    CT = sb["AR1"]
    ct_toks = []
    for half in range(2):
        dcols = slice(512 * half, 512 * (half + 1))
        for out_i, (m1, m2, lh1, lh2) in enumerate((
                ("dft_fr", "dft_fineg", wrT, wiT),
                ("dft_fi", "dft_fr", wrT, wiT))):
            banks = []
            for oc in range(NCD):
                bank, ibk, free = pp.acquire()
                p.wait("tensor", free)
                banks.append((bank, ibk))
            for mi, (mname, lh) in enumerate(((m1, lh1), (m2, lh2))):
                p.wait("sync", ("t", p.cnt["t"]))
                piece_tok = p.dma("sync", "xns", lambda e, mname=mname, dcols=dcols: e.dma_start(
                    out=b.XNS, in_=b.dr[mname][:, dcols].rearrange("(c p) t -> p c t", p=128)))
                p.wait("tensor", piece_tok)
                p.wait("tensor", tk_wr)
                p.wait("tensor", tk_wi)
                for oc in range(NCD):
                    bank, ibk = banks[oc]
                    for c in range(NCD):
                        p.emit("tensor", lambda e, bank=bank, c=c, oc=oc, lh=lh, mi=mi: e.matmul(
                            bank.ap()[:, :], lh[:, c, 128 * oc:128 * (oc + 1)],
                            sb["SCR9"][:, c, :],
                            start=(mi == 0 and c == 0), stop=(mi == 1 and c == NCD - 1)), "t", 1)
            for oc in range(NCD):
                bank, ibk = banks[oc]
                p.wait("vector", ("t", p.cnt["t"]))
                t_ev = p.emit("vector", lambda e, bank=bank, oc=oc, out_i=out_i, dcols=dcols: e.tensor_copy(
                    CT[:, oc, 1024 * out_i + dcols.start:1024 * out_i + dcols.start + 512],
                    bank.ap()[:, :]), "v", 1)
                ct_toks.append(t_ev)
                pp.release(ibk, t_ev)
    m_toks = []
    MV = b.MV
    for half in range(2):
        ocols = slice(512 * half, 512 * (half + 1))
        banks = []
        for dc in range(NCD):
            bank, ibk, free = pp.acquire()
            p.wait("tensor", free)
            banks.append((bank, ibk))
        for mi, (mname, ctoff) in enumerate((("dft_gr", 0), ("dft_gineg", 1024))):
            p.wait("sync", ("t", p.cnt["t"]))
            piece_tok = p.dma("sync", "xns", lambda e, mname=mname, ocols=ocols: e.dma_start(
                out=b.XNS, in_=b.dr[mname][:, ocols].rearrange("(c p) t -> p c t", p=128)))
            p.wait("tensor", piece_tok)
            p.waits("tensor", ct_toks)
            for dc in range(NCD):
                bank, ibk = banks[dc]
                for c in range(NCD):
                    p.emit("tensor", lambda e, bank=bank, c=c, dc=dc, ctoff=ctoff, mi=mi: e.matmul(
                        bank.ap()[:, :], CT[:, c, ctoff + 128 * dc:ctoff + 128 * (dc + 1)],
                        sb["SCR9"][:, c, :],
                        start=(mi == 0 and c == 0), stop=(mi == 1 and c == NCD - 1)), "t", 1)
        for dc in range(NCD):
            bank, ibk = banks[dc]
            p.wait("vector", ("t", p.cnt["t"]))
            t_ev = p.emit("vector", lambda e, bank=bank, dc=dc, ocols=ocols: e.tensor_copy(
                MV[:, 1024 * dc + ocols.start:1024 * dc + ocols.start + 512], bank.ap()[:, :]), "v", 1)
            m_toks.append(t_ev)
            pp.release(ibk, t_ev)
    BIA = sb["ROW"][:, 1536:1544]
    xns_use = []
    new_x = []
    for g in range(8):
        cols = slice(512 * g, 512 * (g + 1))
        xt = b.load_xns(xn_scr, cols, xns_use)
        xns_use = []
        for oc in range(NCD):
            bank, ibk, free = pp.acquire()
            p.wait("tensor", free)
            p.wait("tensor", xt)
            p.waits("tensor", m_toks)
            t_mm = None
            for dc in range(NCD):
                t_mm = p.emit("tensor", lambda e, bank=bank, dc=dc, oc=oc, MV=MV: e.matmul(
                    bank.ap()[:, :], MV[:, 1024 * dc + 128 * oc:1024 * dc + 128 * (oc + 1)],
                    sb["SCR9"][:, dc, :], start=(dc == 0), stop=(dc == NCD - 1)), "t", 1)
            xns_use.append(t_mm)
            p.wait("vector", t_mm)
            p.waits("vector", b.x_toks)
            p.wait("vector", b._bias0_tok)
            t_x = p.emit("vector", lambda e, bank=bank, oc=oc, cols=cols: e.scalar_tensor_tensor(
                sb["X"][:, oc, cols], bank.ap()[:, :], BIA[:, oc:oc + 1],
                sb["X"][:, oc, cols], ALU.add, ALU.add), "v", 1)
            new_x.append(t_x)
            pp.release(ibk, t_x)
    b.x_toks = new_x



def block1(b: Build):
    p, sb, pp, nc = b.p, b.sb, b.pp, b.nc
    xn_scr = nc.dram_tensor("xn_scr1", [NCD, 128, T], BF16).ap()


    AR6 = sb["AR6"]
    tk_aug = p.dma("sync", "bc", lambda e: e.dma_start(out=AR6[:, 0:1024], in_=b.dr["l1_augk"][:, :]))
    tk_msk = p.dma("sync", "bc", lambda e: e.dma_start(out=AR6[:, 1024:5120], in_=b.dr["mask_window"][:, :]))
    tk_oh = p.dma("sync", "bc", lambda e: e.dma_start(out=sb["OH8"][:, :], in_=b.dr["onehot8"][:, :]))
    b.tk_oh8 = tk_oh
    p.wait("gpsimd", ("v", p.cnt["v"]))
    AUGQC = sb["ROW"].ap().bitcast(BF16)[:, 3328:3840]
    tk_aqc = p.dma("gpsimd", "sm", lambda e, AUGQC=AUGQC: e.dma_start(out=AUGQC, in_=b.dr["l1_augqc"][:, :]))

    def mask_at(j):
        return AR6[:, 1024 + 512 * j:1024 + 512 * (j + 1)]

    b.ln_xn(xn_scr)

    GQ = sb["ROW"][:, 1536:1544]
    GK = sb["ROW"][:, 1544:1552]
    tk_gq = p.dma("gpsimd", "sm", lambda e, GQ=GQ: e.dma_start(
        out=GQ, in_=b.dr["L1_attn_q_gamma"].rearrange("(c p) -> p c", p=128),
        allow_slow_non_contiguous=True))
    tk_gk = p.dma("gpsimd", "sm", lambda e, GK=GK: e.dma_start(
        out=GK, in_=b.dr["L1_attn_k_gamma"].rearrange("(c p) -> p c", p=128),
        allow_slow_non_contiguous=True))

    KT = sb["AR1"]
    QT, AOT = b.QT, b.AOT
    PTb = b.PTb
    AUGKH = b.PT23
    slopes8 = [8.0 * 2.0 ** (-0.5 * (h + 1)) for h in range(H)]

    vt_consumers = list(b.vt_consumers)
    kt_consumers = []
    new_x_toks = []
    xns_use_io = [[]]

    def rms_normalize(dst, gam, gam_tok, raw_toks, banks_list, g, sqs=None):
        R0 = sb["ROW"]
        gg = 0 if g is None else g
        if sqs is None:
            sqs = b.SQS

        def dslice(c):
            return dst[:, c, 512 * gg:512 * (gg + 1)]
        raw_ev = []
        for c in range(NCD):
            bank, idx = banks_list[c]
            p.wait("vector", raw_toks[c])
            p.wait("vector", p.stream_last.get("dbg"))
            t_ev = p.emit("vector", lambda e, bank=bank, c=c: e.tensor_copy(
                dslice(c), bank.ap()[:, :]), "v", 1)
            raw_ev.append(t_ev)
            pp.release(idx, t_ev)
        sq_toks = []
        p.wait("scalar", ("t", p.cnt["t"]))
        for c in range(NCD):
            p.wait("scalar", raw_ev[c])
            sq_toks.append(p.emit("scalar", lambda e, c=c, sqs=sqs: e.activation(
                sqs[:, 512 * c:512 * (c + 1)], dslice(c), AF.Square), "s", 1))
        bsum, isum, freeS = pp.acquire()
        p.wait("tensor", freeS)
        p.wait("tensor", b.const_tok["ONEB"])
        p.waits("tensor", sq_toks)
        tokS = None
        for c in range(NCD):
            tokS = p.emit("tensor", lambda e, bsum=bsum, c=c, sqs=sqs: e.matmul(
                bsum.ap()[0:1, :], sb["ONEB"][:, :], sqs[:, 512 * c:512 * (c + 1)],
                start=(c == 0), stop=(c == NCD - 1)), "t", 1)
        p.wait("vector", tokS)
        t_ms = p.emit("vector", lambda e, bsum=bsum: e.tensor_scalar(
            R0[0:1, 0:512], bsum.ap()[0:1, :], 1.0 / D, None, ALU.mult), "v", 1)
        pp.release(isum, t_ms)
        p.wait("scalar", t_ms)
        t_sd = p.emit("scalar", lambda e: e.activation(
            b.TMP_A[0:1, :], R0[0:1, 0:512], AF.Sqrt, bias=1e-6), "s", 1)
        p.wait("vector", t_sd)
        t_r = p.emit("vector", lambda e: e.reciprocal(
            R0[0:1, 512:1024], b.TMP_A[0:1, :]), "v", 1)
        p.wait("tensor", t_r)
        brq, irq, trq = b.bcast_row(R0[0:1, 512:1024])
        evs = []
        for c in range(NCD):
            p.wait("vector", trq)
            p.wait("vector", gam_tok)
            p.wait("vector", sq_toks[c])
            t_ev = p.emit("vector", lambda e, c=c, gam=gam, brq=brq: e.scalar_tensor_tensor(
                dslice(c), dslice(c), gam[:, c:c + 1], brq.ap()[:, :], ALU.mult, ALU.mult), "v", 1)
            evs.append(t_ev)
        pp.release(irq, evs[-1])
        return evs

    for s in range(BPC):
        p.wait("vector", ("t", p.cnt["t"]))
        WvT, tk_wv = b.wslot(0, "L1_attn_wv")
        kt_ev = []
        vt_ev = []
        for g, xns_tok, vt_ev in produce_vt(b, s, xn_scr, WvT, tk_wv, vt_consumers, xns_use_io):
            if g == 0:
                b._wkt, b._tkwk = b.wslot(1, "L1_attn_wk")
            WkT, tk_wk = b._wkt, b._tkwk
            banks_list = []
            raw_toks = []
            for cp in range(NCD):
                bank, ia, freeA = pp.acquire()
                p.wait("tensor", freeA)
                p.wait("tensor", tk_wk)
                p.wait("tensor", xns_tok)
                tA = None
                for c in range(NCD):
                    tA = p.emit("tensor", lambda e, bank=bank, c=c, cp=cp, WkT=WkT: e.matmul(
                        bank.ap()[:, :], WkT[:, c, 128 * cp:128 * (cp + 1)],
                        sb["SCR9"][:, c, :], start=(c == 0), stop=(c == NCD - 1)), "t", 1)
                banks_list.append((bank, ia))
                raw_toks.append(tA)
                xns_use_io[0].append(tA)
            p.waits("vector", kt_consumers)
            p.waits("vector", b.xn_toks)
            kt_ev += rms_normalize(KT, GK, tk_gk, raw_toks, banks_list, g)
        vt_consumers = []
        kt_consumers = []
        if DEBUG_L1 and s == 0:
            dbg_kt = nc.dram_tensor("dbg_kt", [NCD, 128, 2048], BF16).ap()
            for c in range(NCD):
                p.waits("sync", kt_ev)
                p.dma("sync", "dbg", lambda e, c=c, dbg_kt=dbg_kt: e.dma_start(
                    out=dbg_kt[c], in_=KT[:, c, :]))
            b._dbg_kt_done = p.stream_last["dbg"]

        WqT, tk_wq = b.wslot(0, "L1_attn_wq")
        WoT, tk_wo = None, None
        for qt in range(4):
            cols = slice(2048 * s + 512 * qt, 2048 * s + 512 * (qt + 1))
            xns_tok = b.load_xns(xn_scr, cols, xns_use_io[0])
            xns_use_io[0] = []
            banks_list = []
            raw_toks = []
            for cp in range(NCD):
                bank, ia, freeA = pp.acquire()
                p.wait("tensor", freeA)
                p.wait("tensor", tk_wq)
                p.wait("tensor", xns_tok)
                tA = None
                for c in range(NCD):
                    tA = p.emit("tensor", lambda e, bank=bank, c=c, cp=cp, WqT=WqT: e.matmul(
                        bank.ap()[:, :], WqT[:, c, 128 * cp:128 * (cp + 1)],
                        sb["SCR9"][:, c, :], start=(c == 0), stop=(c == NCD - 1)), "t", 1)
                banks_list.append((bank, ia))
                raw_toks.append(tA)
                xns_use_io[0].append(tA)
            qt_toks = rms_normalize(QT, GQ, tk_gq, raw_toks, banks_list, None, sqs=b.f5[:, 4096:8192])
            if DEBUG_L1 and s == 0 and qt == 0:
                dbg_qt = nc.dram_tensor("dbg_qt", [NCD, 128, 512], BF16).ap()
                for c in range(NCD):
                    p.waits("sync", qt_toks)
                    p.dma("sync", "dbg", lambda e, c=c, dbg_qt=dbg_qt: e.dma_start(
                        out=dbg_qt[c], in_=QT[:, c, :]))
            aot_toks = []
            js = [j for j in range(8) if 4 * qt - 4 + j >= 0]
            pt_free = [None, None]
            augkh_busy = None
            for h in range(H):
                p0 = 64 * (h % 2)
                hc = h // 2
                p.wait("gpsimd", augkh_busy)
                p.wait("gpsimd", xns_tok)
                p.waits("gpsimd", xns_use_io[0])
                t_ah = p.dma("gpsimd", "akh", lambda e, p0=p0, h=h: e.dma_start(
                    out=AUGKH[p0:p0 + 3, :], in_=b.dr["l1_augkh"][h]))
                pv_bank, ipv, freepv = pp.acquire()
                p.wait("tensor", freepv)
                sc_banks = [pp.acquire(), pp.acquire()]
                t_pv = None
                t_sc2 = None
                for jj, j in enumerate(js):
                    kc = 4 * qt - 4 + j
                    bank, ibk, freebk = sc_banks[jj % 2]
                    p.wait("tensor", freebk)
                    p.waits("tensor", qt_toks)
                    p.waits("tensor", kt_ev)
                    p.wait("tensor", t_ah)
                    p.wait("tensor", tk_aqc)
                    t_sc = p.emit("tensor", lambda e, bank=bank, p0=p0, hc=hc, kc=kc: e.matmul(
                        bank.ap()[:, :], KT[p0:p0 + 64, hc, 128 * kc:128 * (kc + 1)],
                        QT[p0:p0 + 64, hc, :], start=True, stop=False), "t", 1)
                    t_sc2 = p.emit("tensor", lambda e, bank=bank, p0=p0, j=j, AUGQC=AUGQC: e.matmul(
                        bank.ap()[:, :], AUGKH[p0:p0 + 3, 128 * j:128 * (j + 1)],
                        AUGQC[p0:p0 + 3, :], start=False, stop=True), "t", 1)
                    ptslot = jj % 2
                    p.wait("vector", t_sc2)
                    p.wait("vector", tk_msk)
                    t_mk = p.emit("vector", lambda e, bank=bank, j=j: e.tensor_tensor(
                        bank.ap()[:, :], bank.ap()[:, :], mask_at(j), ALU.min), "v", 1)
                    p.wait("scalar", t_mk)
                    p.wait("scalar", pt_free[ptslot])
                    t_exp = p.emit("scalar", lambda e, bank=bank, ptslot=ptslot: e.activation(
                        PTb[ptslot], bank.ap()[:, :], AF.Exp, scale=0.125), "s", 1)
                    sc_banks[jj % 2] = (bank, ibk, t_exp)
                    t_p = t_exp
                    p.wait("tensor", t_p)
                    p.waits("tensor", vt_ev)
                    t_pv = p.emit("tensor", lambda e, pv_bank=pv_bank, kc=kc, h=h, ptslot=ptslot, jj=jj, njs=len(js): e.matmul(
                        pv_bank.ap()[0:65, :], mkap(sb["AR2"], 1040 * kc + 65 * h, [[1, 65]]),
                        PTb[ptslot], start=(jj == 0), stop=(jj == njs - 1)), "t", 1)
                    vt_consumers.append(t_pv)
                    pt_free[ptslot] = t_pv
                augkh_busy = t_sc2
                for bank, ibk, tokk in sc_banks:
                    pp.release(ibk, tokk)
                head_epilogue(b, t_pv, pv_bank, ipv, p0, hc, aot_toks)
            if DEBUG_L1 and s == 0 and qt == 0:
                dbg_ao = nc.dram_tensor("dbg_ao", [NCD, 128, 512], BF16).ap()
                for c in range(NCD):
                    p.waits("sync", aot_toks)
                    p.dma("sync", "dbg", lambda e, c=c, dbg_ao=dbg_ao: e.dma_start(
                        out=dbg_ao[c], in_=AOT[:, c, :]))
                b._dbg_ao_done = p.stream_last["dbg"]
            if WoT is None:
                WoT, tk_wo = b.wslot(1, "L1_attn_wo")
            wo_project(b, WoT, tk_wo, aot_toks, cols, new_x_toks)
    b.vt_consumers = vt_consumers
    b.x_toks = b.x_toks + new_x_toks

    l1_ffn(b, xn_scr, tk_oh)


def l1_ffn(b: Build, xn_scr, tk_oh):
    p, sb, pp, nc = b.p, b.sb, b.pp, b.nc
    AR3h = sb["AR3"]
    R0 = sb["ROW"]
    OUT_G = sb["AR5"].ap().rearrange("p a c t -> p (a c t)").bitcast(F32)
    ROWB = R0.ap().bitcast(BF16)
    MSK = ROWB[0:8, 3328:3840]
    MEQ = ROWB[:, 4002:4018]
    HBUF = sb["AR1"].ap().rearrange("p c t -> p (c t)")

    p.wait("gpsimd", ("t", p.cnt["t"]))
    p.wait("gpsimd", ("v", p.cnt["v"]))
    WRS = ROWB[:, 4032:4096].rearrange("p (c e) -> p c e", e=8)
    tk_wr = p.dma("gpsimd", "sm", lambda e, WRS=WRS: e.dma_start(
        out=WRS, in_=b.dr["L1_ffn_Wr"].rearrange("(c p) e -> p c e", p=128),
        allow_slow_non_contiguous=True))
    BRR = R0[0:1, 1552:1560]
    tk_br = p.dma("gpsimd", "sm", lambda e, BRR=BRR: e.dma_start(
        out=BRR, in_=b.dr["L1_ffn_br"].rearrange("(a e) -> a e", a=1)))
    B2A = R0[:, 1928:1992]
    B2S = R0[:, 1992:2000]
    tk_b2 = p.dma("gpsimd", "sm", lambda e, B2A=B2A: e.dma_start(
        out=B2A.rearrange("p (e c) -> p e c", e=8),
        in_=b.dr["L1_ffn_b2"].rearrange("e (c p) -> p e c", p=128),
        allow_slow_non_contiguous=True))
    p.wait("vector", tk_b2)
    t_b2s = p.emit("vector", lambda e, B2A=B2A, B2S=B2S: e.tensor_reduce(
        B2S.unsqueeze(2),
        B2A.rearrange("p (e c) -> p e c", e=8).transpose([0, 2, 1]),
        AX.X, ALU.add), "v", 1)

    b1_busy = None
    out_evac_toks = list(b.x_toks)
    slot_busy = {0: None, 1: None, 2: None, 3: None}
    xns_use = []
    for g in range(8):
        cols = slice(512 * g, 512 * (g + 1))
        p.wait("vector", ("t", p.cnt["t"]))
        p.wait("vector", ("v", p.cnt["v"]))
        t_z = p.emit("vector", lambda e, OUT_G=OUT_G: e.memset(OUT_G, 0.0), "v", 1)
        xns_tok = b.load_xns(xn_scr, cols, xns_use)
        xns_use = []
        msk_toks = []
        for sub in range(4):
            bank, ibk, free = pp.acquire()
            p.wait("tensor", free)
            p.wait("tensor", tk_wr)
            p.wait("tensor", tk_br)
            p.wait("tensor", xns_tok)
            t_mm = None
            for c in range(NCD):
                t_mm = p.emit("tensor", lambda e, bank=bank, c=c, sub=sub, WRS=WRS: e.matmul(
                    bank.ap()[0:128, 0:8], sb["SCR9"][:, c, 128 * sub:128 * (sub + 1)],
                    WRS[:, c, :], start=(c == 0), stop=False), "t", 1)
            t_mm = p.emit("tensor", lambda e, bank=bank, BRR=BRR: e.matmul(
                bank.ap()[0:128, 0:8], sb["ONER"][:, 0:128],
                BRR, start=False, stop=True), "t", 1)
            xns_use.append(t_mm)
            p.wait("vector", t_mm)
            t_mx = p.emit("vector", lambda e, bank=bank: e.reduce_max(
                R0[:, 2000:2001], bank.ap()[0:128, 0:8], AX.X), "v", 1)
            p.wait("vector", t_mx)
            t_eq = p.emit("vector", lambda e, bank=bank, MEQ=MEQ: e.tensor_scalar(
                MEQ[:, 0:8], bank.ap()[0:128, 0:8], R0[:, 2000:2001], None,
                ALU.is_equal), "v", 1)
            pp.release(ibk, t_eq)
            tile, pi, free2 = pp.acquire()
            tb = tile.ap().bitcast(BF16)
            p.wait("tensor", free2)
            p.wait("tensor", t_eq)
            t_tp = p.emit("tensor", lambda e, tb=tb, MEQ=MEQ: e.transpose(
                tb[0:8, 0:128], MEQ[:, 0:8], sb["IDB"][:, :]), "t", 1)
            p.wait("vector", t_tp)
            t_m = p.emit("vector", lambda e, tb=tb, sub=sub, MSK=MSK: e.tensor_copy(
                MSK[:, 128 * sub:128 * (sub + 1)], tb[0:8, 0:128]), "v", 1)
            msk_toks.append(t_m)
            pp.release(pi, t_m)
        for ex in range(8):
            xns_tok = b.load_xns(xn_scr, cols, xns_use)
            xns_use = []
            p.wait("gpsimd", b1_busy)
            tk_b1 = p.dma("gpsimd", "sm", lambda e, ex=ex: e.dma_start(
                out=R0[:, 1560:1592], in_=b.dr["L1_ffn_b1"][ex].rearrange("(c p) -> p c", p=128),
                allow_slow_non_contiguous=True))
            mb, imb, freemb = pp.acquire()
            p.wait("tensor", freemb)
            p.waits("tensor", msk_toks)
            p.wait("tensor", tk_oh)
            t_mb = p.emit("tensor", lambda e, mb=mb, ex=ex, MSK=MSK: e.matmul(
                mb.ap()[:, :], sb["OH8"][:, 128 * ex:128 * (ex + 1)], MSK[:, :],
                start=True, stop=True), "t", 1)
            xi_toks = []
            for c in range(NCD):
                p.wait("vector", t_mb)
                p.wait("vector", xns_tok)
                t_xi = p.emit("vector", lambda e, c=c, mb=mb: e.tensor_tensor(
                    sb["SCR9"][:, c, :], sb["SCR9"][:, c, :], mb.ap()[:, :], ALU.mult), "v", 1)
                xi_toks.append(t_xi)
            pp.release(imb, xi_toks[-1])
            xns_use += xi_toks
            h_toks = []
            scr1, rtok1 = b.scr["L1_ffn_W1"]
            for hq in range(8):
                half = hq % 2
                p.wait("sync", rtok1)
                p.wait("sync", slot_busy[half])
                dst = mkap(AR3h, 4096 * half, [[512, 8], [1, 512]])
                tk_w1 = p.dma("sync", f"l1w{half}", lambda e, ex=ex, hq=hq, dst=dst, scr1=scr1: e.dma_start(
                    out=dst, in_=scr1[ex].rearrange("(c p) h -> p c h", p=128)[:, :, 512 * hq:512 * (hq + 1)]))
                t_mm = None
                for hcp in range(4):
                    hc = 4 * hq + hcp
                    bank, ibk, free = pp.acquire()
                    p.wait("tensor", free)
                    p.wait("tensor", tk_w1)
                    for c in range(NCD):
                        p.wait("tensor", xi_toks[c])
                        t_mm = p.emit("tensor", lambda e, bank=bank, c=c, half=half, hcp=hcp, AR3h=AR3h: e.matmul(
                            bank.ap()[:, :], mkap(AR3h, 4096 * half + 512 * c + 128 * hcp, [[1, 128]]),
                            sb["SCR9"][:, c, :], start=(c == 0), stop=(c == NCD - 1)), "t", 1)
                    xns_use.append(t_mm)
                    p.wait("scalar", t_mm)
                    p.wait("scalar", tk_b1)
                    t_h = p.emit("scalar", lambda e, bank=bank, hc=hc, HBUF=HBUF: e.activation(
                        HBUF[:, 512 * hc:512 * (hc + 1)], bank.ap()[:, :], AF.Relu,
                        bias=R0[:, 1560 + hc:1561 + hc], scale=1.0), "s", 1)
                    h_toks.append(t_h)
                    pp.release(ibk, t_h)
                slot_busy[half] = t_mm
            b1_busy = h_toks[-1]
            oc_banks = []
            for oc in range(NCD):
                bank, ibk, free = pp.acquire()
                p.wait("tensor", free)
                oc_banks.append((bank, ibk))
            t_w2last = None
            scr2, rtok2 = b.scr["L1_ffn_W2"]
            for q8 in range(8):
                half = q8 % 2
                p.wait("sync", rtok2)
                p.wait("sync", slot_busy[2 + half])
                dst = mkap(AR3h, 8192 + 4096 * half, [[1024, 4], [1, 1024]])
                tk_w2 = p.dma("sync", f"l1v{half}", lambda e, ex=ex, q8=q8, dst=dst, scr2=scr2: e.dma_start(
                    out=dst, in_=scr2[ex].rearrange("(c p) o -> p c o", p=128)[:, 4 * q8:4 * q8 + 4, :]))
                for icp in range(4):
                    ic = 4 * q8 + icp
                    p.wait("tensor", tk_w2)
                    p.waits("tensor", h_toks)
                    for oc in range(NCD):
                        bank, ibk = oc_banks[oc]
                        t_w2last = p.emit("tensor", lambda e, bank=bank, half=half, icp=icp, oc=oc, ic=ic, q8=q8, AR3h=AR3h, HBUF=HBUF: e.matmul(
                            bank.ap()[:, :], mkap(AR3h, 8192 + 4096 * half + 1024 * icp + 128 * oc, [[1, 128]]),
                            HBUF[:, 512 * ic:512 * (ic + 1)],
                            start=(q8 == 0 and icp == 0), stop=(q8 == 7 and icp == 3)), "t", 1)
                    if icp == 3:
                        slot_busy[2 + half] = t_w2last
            for oc in range(NCD):
                bank, ibk = oc_banks[oc]
                p.wait("vector", ("t", p.cnt["t"]))
                p.wait("vector", t_z)
                t_acc = p.emit("vector", lambda e, bank=bank, oc=oc, OUT_G=OUT_G: e.tensor_tensor(
                    OUT_G[:, 512 * oc:512 * (oc + 1)], bank.ap()[:, :],
                    OUT_G[:, 512 * oc:512 * (oc + 1)], ALU.add), "v", 1)
                pp.release(ibk, t_acc)
        p.wait("vector", ("v", p.cnt["v"]))
        for oc in range(NCD):
            p.wait("vector", t_b2s)
            p.waits("vector", out_evac_toks)
            t_x = p.emit("vector", lambda e, oc=oc, cols=cols, OUT_G=OUT_G, B2S=B2S: e.scalar_tensor_tensor(
                sb["X"][:, oc, cols], OUT_G[:, 512 * oc:512 * (oc + 1)], B2S[:, oc:oc + 1],
                sb["X"][:, oc, cols], ALU.add, ALU.add), "v", 1)
            out_evac_toks.append(t_x)
    b.x_toks = out_evac_toks



def qstk_ap(AR1h, h, r0, nr):
    fs = 1
    for d in AR1h.shape[1:]:
        fs *= d
    return bass.AP(tensor=AR1h, offset=512 * h, ap=[[fs, 128], [1, 512]])[r0:r0 + nr, :]


def block2(b: Build):
    p, sb, pp, nc = b.p, b.sb, b.pp, b.nc
    xn_scr = nc.dram_tensor("xn_scr2", [NCD, 128, T], BF16).ap()
    kstk_scr = nc.dram_tensor("kstk_scr", [H, 128, S], BF16).ap()


    AR6 = sb["AR6"]
    AR1h = sb["AR1"]
    tk_e16 = p.dma("sync", "bc", lambda e: e.dma_start(out=AR6[:, 0:128], in_=b.dr["e16sel"][:, :]))
    tk_selh = p.dma("sync", "bc", lambda e: e.dma_start(out=AR6[0:16, 128:1152], in_=b.dr["selh"][:, :]))
    tk_msk = p.dma("sync", "bc", lambda e: e.dma_start(out=AR6[:, 2048:4096], in_=b.dr["mask_causal"][:, :]))

    def mask_at(j):
        return AR6[:, 2048 + 512 * j:2048 + 512 * (j + 1)]

    b.ln_xn(xn_scr)

    R0 = sb["ROW"]
    p.wait("gpsimd", ("v", p.cnt["v"]))
    p.wait("gpsimd", ("s", p.cnt["s"]))
    tk_sls = p.dma("gpsimd", "sm", lambda e: e.dma_start(
        out=R0[0:1, 1552:1568], in_=b.dr["L2_attn_score_log_scale"].rearrange("(a e) -> a e", a=1)))
    p.wait("scalar", tk_sls)
    t_se = p.emit("scalar", lambda e: e.activation(
        R0[0:1, 1568:1584], R0[0:1, 1552:1568], AF.Exp), "s", 1)
    p.wait("tensor", t_se)
    bsl, isl, freesl = pp.acquire()
    p.wait("tensor", freesl)
    t_slb = p.emit("tensor", lambda e, bsl=bsl: e.matmul(
        bsl.ap()[0:128, 0:16], b.sb["ONER"][:, 0:128], R0[0:1, 1568:1584],
        start=True, stop=True), "t", 1)
    SLSB = R0[:, 1536:1552]
    p.wait("scalar", t_slb)
    t_slsb = p.emit("scalar", lambda e, bsl=bsl, SLSB=SLSB: e.activation(
        SLSB, bsl.ap()[0:128, 0:16], AF.Copy, scale=0.125), "s", 1)
    pp.release(isl, t_slsb)

    QT, AOT = b.QT, b.AOT
    PTb = b.PTb

    def b1c(c):
        return mkap(AR1h, 8192 + 512 * c, [[1, 512]])

    def b2c(c):
        return mkap(AR1h, 12288 + 512 * c, [[1, 512]])

    def b3c(c):
        return b.f5[:, 512 * c:512 * (c + 1)]

    def aotc(c):
        return b.f9[:, 512 * c:512 * (c + 1)]

    def kbuf(i, cols):
        return mkap(AR1h, 8192 + 2048 * i, [[1, cols]])

    vt_consumers = list(b.vt_consumers)
    new_x_toks = []
    xns_use_io = [[]]
    b._l2_sin_prev = None

    def ampphi_produce(Wamp, tk_wa, Wphi, tk_wp, xns_tok, cos_dst, sin_dst, dst_toks,
                       extra_dst_waits, extra_b3_waits=()):
        amp_toks = []
        for cp in range(NCD):
            bank, ia, freeA = pp.acquire()
            p.wait("tensor", freeA)
            p.wait("tensor", tk_wa)
            p.wait("tensor", xns_tok)
            tA = None
            for c in range(NCD):
                tA = p.emit("tensor", lambda e, bank=bank, c=c, cp=cp, Wamp=Wamp: e.matmul(
                    bank.ap()[:, :], Wamp[:, c, 128 * cp:128 * (cp + 1)],
                    sb["SCR9"][:, c, :], start=(c == 0), stop=(c == NCD - 1)), "t", 1)
            xns_use_io[0].append(tA)
            p.wait("scalar", tA)
            p.wait("scalar", ("v", p.cnt["v"]))
            p.wait("scalar", p.stream_last.get("kb0"))
            p.wait("scalar", p.stream_last.get("kb1"))
            t_e = p.emit("scalar", lambda e, bank=bank, cp=cp: e.activation(
                b1c(cp), bank.ap()[:, :], AF.Exp), "s", 1)
            pp.release(ia, t_e)
            amp_toks.append(t_e)
        ln_toks = []
        for cp in range(NCD):
            p.wait("scalar", amp_toks[cp])
            ln_toks.append(p.emit("scalar", lambda e, cp=cp: e.activation(
                b2c(cp), b1c(cp), AF.Ln, bias=1.0), "s", 1))
        sq_toks = []
        for cp in range(NCD):
            p.wait("scalar", ln_toks[cp])
            sq_toks.append(p.emit("scalar", lambda e, cp=cp: e.activation(
                b1c(cp), b2c(cp), AF.Square), "s", 1))
        bms, ims, freems = pp.acquire()
        p.wait("tensor", freems)
        p.wait("tensor", tk_e16)
        p.waits("tensor", sq_toks)
        tok16 = None
        for cp in range(NCD):
            tok16 = p.emit("tensor", lambda e, bms=bms, cp=cp: e.matmul(
                bms.ap()[0:16, :], AR6[:, 16 * cp:16 * (cp + 1)], b1c(cp),
                start=(cp == 0), stop=(cp == NCD - 1)), "t", 1)
        p.wait("vector", tok16)
        t_ms = p.emit("vector", lambda e, bms=bms: e.tensor_scalar(
            R0[0:16, 0:512], bms.ap()[0:16, :], 1.0 / HD, None, ALU.mult), "v", 1)
        pp.release(ims, t_ms)
        p.wait("scalar", t_ms)
        t_sd = p.emit("scalar", lambda e: e.activation(
            b.TMP_A[0:16, :], R0[0:16, 0:512], AF.Sqrt, bias=1e-6), "s", 1)
        p.wait("vector", t_sd)
        RB = R0.ap().bitcast(BF16)
        t_r0 = p.emit("vector", lambda e: e.reciprocal(
            b.TMP_B[0:16, :], b.TMP_A[0:16, :]), "v", 1)
        p.wait("vector", t_r0)
        t_r = p.emit("vector", lambda e, RB=RB: e.tensor_copy(
            RB[0:16, 1024:1536], b.TMP_B[0:16, :]), "v", 1)
        amp2 = []
        for cp in range(NCD):
            bbc, ibc, freebc = pp.acquire()
            p.wait("tensor", freebc)
            p.wait("tensor", t_r)
            p.wait("tensor", tk_selh)
            t_b1 = p.emit("tensor", lambda e, bbc=bbc, cp=cp, RB=RB: e.matmul(
                bbc.ap()[:, :], AR6[0:16, 128 + 128 * cp:128 + 128 * (cp + 1)],
                RB[0:16, 1024:1536], start=True, stop=True), "t", 1)
            p.wait("vector", t_b1)
            p.wait("vector", ("s", p.cnt["s"]))
            t_a2 = p.emit("vector", lambda e, bbc=bbc, cp=cp: e.tensor_tensor(
                b2c(cp), b2c(cp), bbc.ap()[:, :], ALU.mult), "v", 1)
            pp.release(ibc, t_a2)
            amp2.append(t_a2)
        tanh_toks = []
        for cp in range(NCD):
            bank, ia, freeA = pp.acquire()
            p.wait("tensor", freeA)
            p.wait("tensor", tk_wp)
            p.wait("tensor", xns_tok)
            tA = None
            for c in range(NCD):
                tA = p.emit("tensor", lambda e, bank=bank, c=c, cp=cp, Wphi=Wphi: e.matmul(
                    bank.ap()[:, :], Wphi[:, c, 128 * cp:128 * (cp + 1)],
                    sb["SCR9"][:, c, :], start=(c == 0), stop=(c == NCD - 1)), "t", 1)
            xns_use_io[0].append(tA)
            p.wait("scalar", tA)
            p.waits("scalar", sq_toks)
            p.wait("scalar", ("t", p.cnt["t"]))
            t_t = p.emit("scalar", lambda e, bank=bank, cp=cp: e.activation(
                b1c(cp), bank.ap()[:, :], AF.Tanh), "s", 1)
            pp.release(ia, t_t)
            tanh_toks.append(t_t)
        s2_toks = []
        for cp in range(NCD):
            p.wait("scalar", tanh_toks[cp])
            p.wait("scalar", b._l2_sin_prev)
            p.wait("scalar", ("t", p.cnt["t"]))
            p.wait("scalar", ("v", p.cnt["v"]))
            p.waits("scalar", list(extra_b3_waits))
            s2_toks.append(p.emit("scalar", lambda e, cp=cp: e.activation(
                b3c(cp), b1c(cp), AF.Sin, scale=math.pi / 2), "s", 1))
        for cp in range(NCD):
            p.wait("scalar", s2_toks[cp])
            p.wait("scalar", b._l2_sin_prev)
            t_s4 = p.emit("scalar", lambda e, cp=cp: e.activation(
                b.QTM, b1c(cp), AF.Sin, scale=math.pi / 4), "s", 1)
            p.wait("scalar", t_s4)
            t_s4q = p.emit("scalar", lambda e: e.activation(
                b.TMP_B, b.QTM, AF.Square), "s", 1)
            p.wait("vector", t_s4q)
            t_c2 = p.emit("vector", lambda e: e.tensor_scalar(
                b.TMP_B, b.TMP_B, -4.0, 2.0, ALU.mult, ALU.add), "v", 1)
            p.wait("scalar", t_c2)
            t_s2q = p.emit("scalar", lambda e, cp=cp: e.activation(
                b.TMP_A, b3c(cp), AF.Square), "s", 1)
            p.wait("vector", t_s2q)
            t_cos = p.emit("vector", lambda e: e.tensor_scalar(
                b.TMP_A, b.TMP_A, -2.0, 1.0, ALU.mult, ALU.add), "v", 1)
            p.wait("vector", t_c2)
            t_sin = p.emit("vector", lambda e, cp=cp: e.tensor_tensor(
                b.QTM, b3c(cp), b.TMP_B, ALU.mult), "v", 1)
            p.wait("vector", t_cos)
            p.wait("vector", amp2[cp])
            p.waits("vector", extra_dst_waits)
            p.wait("vector", ("t", p.cnt["t"]))
            t_dc = p.emit("vector", lambda e, cp=cp, d=cos_dst(cp): e.tensor_tensor(
                d, b2c(cp), b.TMP_A, ALU.mult), "v", 1)
            p.wait("vector", t_sin)
            t_ds = p.emit("vector", lambda e, cp=cp, d=sin_dst(cp): e.tensor_tensor(
                d, b2c(cp), b.QTM, ALU.mult), "v", 1)
            b._l2_sin_prev = t_ds
            dst_toks.append((t_dc, t_ds))

    for s in range(BPC):
        p.wait("vector", ("t", p.cnt["t"]))
        WvT, tk_wv = b.wslot(0, "L2_attn_Wv")
        vt_ev = []
        for g, xns_tok, vt_ev in produce_vt(b, s, xn_scr, WvT, tk_wv, vt_consumers, xns_use_io):
            pass
        vt_consumers = []
        WkaT, tk_ka = b.wslot(0, "L2_attn_Wk_amp")
        WkpT, tk_kp = b.wslot(1, "L2_attn_Wk_phi")
        kst_toks = []
        for g in range(4):
            cols = slice(2048 * s + 512 * g, 2048 * s + 512 * (g + 1))
            xns_tok = b.load_xns(xn_scr, cols, xns_use_io[0])
            xns_use_io[0] = []
            pair_toks = []
            ampphi_produce(WkaT, tk_ka, WkpT, tk_kp, xns_tok,
                           aotc, lambda cp: b3c(cp), pair_toks,
                           [p.stream_last.get("kst")],
                           extra_b3_waits=[p.stream_last.get("kst")])
            for cp in range(NCD):
                t_dc, t_ds = pair_toks[cp]
                for half, hh in ((0, 2 * cp), (1, 2 * cp + 1)):
                    p.wait("sync", t_dc)
                    kst_toks.append(p.dma("sync", "kst", lambda e, cp=cp, half=half, hh=hh, g=g, s=s: e.dma_start(
                        out=kstk_scr[hh, 0:64, 512 * g:512 * (g + 1)],
                        in_=aotc(cp)[64 * half:64 * half + 64, :])))
                for half, hh in ((0, 2 * cp), (1, 2 * cp + 1)):
                    p.wait("sync", t_ds)
                    kst_toks.append(p.dma("sync", "kst", lambda e, cp=cp, half=half, hh=hh, g=g, s=s: e.dma_start(
                        out=kstk_scr[hh, 64:128, 512 * g:512 * (g + 1)],
                        in_=b3c(cp)[64 * half:64 * half + 64, :])))

        WqpT, tk_wqp = b.wslot(1, "L2_attn_Wq_phi")
        for qt in range(4):
            cols = slice(2048 * s + 512 * qt, 2048 * s + 512 * (qt + 1))
            WqaT, tk_wqa = b.wslot(0, "L2_attn_Wq_amp")
            xns_tok = b.load_xns(xn_scr, cols, xns_use_io[0])
            xns_use_io[0] = []
            qt_toks = []
            pair_toks = []
            ampphi_produce(WqaT, tk_wqa, WqpT, tk_wqp, xns_tok,
                           aotc, lambda cp: b3c(cp), pair_toks, [],
                           extra_b3_waits=[p.stream_last.get("kst")])
            for cp in range(NCD):
                t_dc, t_ds = pair_toks[cp]
                for half, hh in ((0, 2 * cp), (1, 2 * cp + 1)):
                    p.wait("vector", t_dc)
                    qt_toks.append(p.emit("vector", lambda e, half=half, hh=hh, cp=cp, AR1h=AR1h: e.tensor_copy(
                        qstk_ap(AR1h, hh, 0, 64), aotc(cp)[64 * half:64 * half + 64, :]), "v", 1))
                for half, hh in ((0, 2 * cp), (1, 2 * cp + 1)):
                    p.wait("vector", t_ds)
                    qt_toks.append(p.emit("vector", lambda e, half=half, hh=hh, cp=cp, AR1h=AR1h: e.tensor_copy(
                        qstk_ap(AR1h, hh, 64, 64), b3c(cp)[64 * half:64 * half + 64, :]), "v", 1))
            aot_toks = []
            n_kc = 4 * qt + 4
            pt_free = [None, None, None, None]
            kb_busy = [None, None]
            for h in range(H):
                kb = h % 2
                ncols = 512 * (qt + 1)
                p.waits("sync", kst_toks)
                p.wait("sync", kb_busy[kb])
                p.wait("sync", ("s", p.cnt["s"]))
                p.wait("sync", ("v", p.cnt["v"]))
                tk_kb = p.dma("sync", f"kb{kb}", lambda e, h=h, kb=kb, ncols=ncols: e.dma_start(
                    out=kbuf(kb, ncols), in_=kstk_scr[h, :, 0:ncols]))
                pv_bank, ipv, freepv = pp.acquire()
                p.wait("tensor", freepv)
                sc_banks = [pp.acquire(), pp.acquire()]
                t_pv = None
                for kc in range(n_kc):
                    bank, ibk, freebk = sc_banks[kc % 2]
                    p.wait("tensor", freebk)
                    p.waits("tensor", qt_toks)
                    p.wait("tensor", tk_kb)
                    t_sc = p.emit("tensor", lambda e, bank=bank, h=h, kc=kc, kb=kb, ncols=ncols, AR1h=AR1h: e.matmul(
                        bank.ap()[:, :], kbuf(kb, ncols)[:, 128 * kc:128 * (kc + 1)],
                        qstk_ap(AR1h, h, 0, 128), start=True, stop=True), "t", 1)
                    kb_busy[kb] = t_sc
                    ptslot = kc % 4
                    p.wait("scalar", t_sc)
                    p.wait("scalar", pt_free[ptslot])
                    p.wait("scalar", t_slsb)
                    t_exp = p.emit("scalar", lambda e, bank=bank, ptslot=ptslot, h=h, SLSB=SLSB: e.activation(
                        PTb[ptslot], bank.ap()[:, :], AF.Exp, scale=SLSB[:, h:h + 1]), "s", 1)
                    sc_banks[kc % 2] = (bank, ibk, t_exp)
                    t_p = t_exp
                    if kc >= 4 * qt:
                        j = kc - 4 * qt
                        p.wait("vector", t_exp)
                        p.wait("vector", tk_msk)
                        t_p = p.emit("vector", lambda e, ptslot=ptslot, j=j: e.tensor_tensor(
                            PTb[ptslot], PTb[ptslot], mask_at(j), ALU.mult), "v", 1)
                    p.wait("tensor", t_p)
                    p.waits("tensor", vt_ev)
                    t_pv = p.emit("tensor", lambda e, pv_bank=pv_bank, kc=kc, h=h, ptslot=ptslot, n_kc=n_kc: e.matmul(
                        pv_bank.ap()[0:65, :], mkap(sb["AR2"], 1040 * kc + 65 * h, [[1, 65]]),
                        PTb[ptslot], start=(kc == 0), stop=(kc == n_kc - 1)), "t", 1)
                    vt_consumers.append(t_pv)
                    pt_free[ptslot] = t_pv
                for bank, ibk, tokk in sc_banks:
                    pp.release(ibk, tokk)
                head_epilogue(b, t_pv, pv_bank, ipv, 64 * (h % 2), h // 2, aot_toks)
            WoT, tk_wo = b.wslot(0, "L2_attn_Wo")
            wo_project(b, WoT, tk_wo, aot_toks, cols, new_x_toks)
    b.vt_consumers = vt_consumers
    b.x_toks = b.x_toks + new_x_toks

    l2_ffn(b, xn_scr)


def l2_ffn(b: Build, xn_scr):
    p, sb, pp, nc = b.p, b.sb, b.pp, b.nc
    AR3h = sb["AR3"]
    R0 = sb["ROW"]
    OUT_G = sb["AR5"].ap().rearrange("p a c t -> p (a c t)").bitcast(F32)
    ROWB = R0.ap().bitcast(BF16)
    CFM = ROWB[0:8, 3328:3840]
    CEQ = ROWB[:, 4002:4018]
    TA = R0[:, 0:64]

    p.wait("gpsimd", ("t", p.cnt["t"]))
    p.wait("gpsimd", ("v", p.cnt["v"]))
    WGS = ROWB[:, 4032:4096].rearrange("p (c e) -> p c e", e=8)
    tk_wg = p.dma("gpsimd", "sm", lambda e, WGS=WGS: e.dma_start(
        out=WGS, in_=b.dr["L2_ffn_Wg"].rearrange("(c p) e -> p c e", p=128),
        allow_slow_non_contiguous=True))

    out_evac_toks = list(b.x_toks)
    slot_busy = {0: None, 1: None}
    xns_use = []
    for g in range(8):
        cols = slice(512 * g, 512 * (g + 1))
        p.wait("vector", ("t", p.cnt["t"]))
        p.wait("vector", ("v", p.cnt["v"]))
        t_z = p.emit("vector", lambda e, OUT_G=OUT_G: e.memset(OUT_G, 0.0), "v", 1)
        xns_tok = b.load_xns(xn_scr, cols, xns_use)
        xns_use = []
        cf_toks = []
        for sub in range(4):
            bank, ibk, free = pp.acquire()
            p.wait("tensor", free)
            p.wait("tensor", tk_wg)
            p.wait("tensor", xns_tok)
            t_mm = None
            for c in range(NCD):
                t_mm = p.emit("tensor", lambda e, bank=bank, c=c, sub=sub, WGS=WGS: e.matmul(
                    bank.ap()[0:128, 0:8], sb["SCR9"][:, c, 128 * sub:128 * (sub + 1)],
                    WGS[:, c, :], start=(c == 0), stop=(c == NCD - 1)), "t", 1)
            xns_use.append(t_mm)
            p.wait("vector", t_mm)
            p.wait("vector", ("s", p.cnt["s"]))
            t_mx = p.emit("vector", lambda e, bank=bank: e.reduce_max(
                TA[:, 48:49], bank.ap()[0:128, 0:8], AX.X), "v", 1)
            p.wait("vector", t_mx)
            t_nm = p.emit("vector", lambda e: e.tensor_scalar(
                TA[:, 49:50], TA[:, 48:49], -1.0, None, ALU.mult), "v", 1)
            p.wait("scalar", t_nm)
            t_ee = p.emit("scalar", lambda e, bank=bank: e.activation(
                TA[:, 0:8], bank.ap()[0:128, 0:8], AF.Exp, bias=TA[:, 49:50]), "s", 1)
            pp.release(ibk, t_ee)
            p.wait("vector", t_ee)
            t_sm = p.emit("vector", lambda e: e.reduce_sum(
                TA[:, 50:51], TA[:, 0:8], AX.X), "v", 1)
            p.wait("vector", t_sm)
            t_rs = p.emit("vector", lambda e: e.reciprocal(
                TA[:, 51:52], TA[:, 50:51]), "v", 1)
            p.wait("vector", t_rs)
            t_g = p.emit("vector", lambda e: e.tensor_scalar(
                TA[:, 8:16], TA[:, 0:8], TA[:, 51:52], None, ALU.mult), "v", 1)
            p.wait("vector", t_g)
            t_m1 = p.emit("vector", lambda e: e.reduce_max(
                TA[:, 52:53], TA[:, 8:16], AX.X), "v", 1)
            p.wait("vector", t_m1)
            t_k1 = p.emit("vector", lambda e: e.tensor_scalar(
                TA[:, 16:24], TA[:, 8:16], TA[:, 52:53], None, ALU.is_equal), "v", 1)
            p.wait("vector", t_k1)
            t_g2 = p.emit("vector", lambda e: e.tensor_tensor(
                TA[:, 24:32], TA[:, 8:16], TA[:, 16:24], ALU.subtract), "v", 1)
            p.wait("vector", t_g2)
            t_m2 = p.emit("vector", lambda e: e.reduce_max(
                TA[:, 53:54], TA[:, 24:32], AX.X), "v", 1)
            p.wait("vector", t_m2)
            t_k2 = p.emit("vector", lambda e: e.tensor_scalar(
                TA[:, 32:40], TA[:, 24:32], TA[:, 53:54], None, ALU.is_equal), "v", 1)
            t_s12 = p.emit("vector", lambda e: e.tensor_tensor(
                TA[:, 54:55], TA[:, 52:53], TA[:, 53:54], ALU.add), "v", 1)
            p.wait("vector", t_s12)
            t_s12b = p.emit("vector", lambda e: e.tensor_scalar(
                TA[:, 54:55], TA[:, 54:55], 1e-9, None, ALU.add), "v", 1)
            p.wait("vector", t_s12b)
            t_rw = p.emit("vector", lambda e: e.reciprocal(
                TA[:, 55:56], TA[:, 54:55]), "v", 1)
            p.wait("vector", t_rw)
            p.wait("vector", t_k2)
            t_c1 = p.emit("vector", lambda e: e.tensor_scalar(
                TA[:, 16:24], TA[:, 16:24], TA[:, 52:53], None, ALU.mult), "v", 1)
            t_c2 = p.emit("vector", lambda e: e.tensor_scalar(
                TA[:, 32:40], TA[:, 32:40], TA[:, 53:54], None, ALU.mult), "v", 1)
            p.wait("vector", t_c2)
            t_cc = p.emit("vector", lambda e: e.tensor_tensor(
                TA[:, 16:24], TA[:, 16:24], TA[:, 32:40], ALU.add), "v", 1)
            p.wait("vector", t_cc)
            t_cf = p.emit("vector", lambda e, CEQ=CEQ: e.tensor_scalar(
                CEQ[:, 0:8], TA[:, 16:24], TA[:, 55:56], None, ALU.mult), "v", 1)
            tile, pi, free2 = pp.acquire()
            tb = tile.ap().bitcast(BF16)
            p.wait("tensor", free2)
            p.wait("tensor", t_cf)
            t_tp = p.emit("tensor", lambda e, tb=tb, CEQ=CEQ: e.transpose(
                tb[0:8, 0:128], CEQ[:, 0:8], sb["IDB"][:, :]), "t", 1)
            p.wait("vector", t_tp)
            t_m = p.emit("vector", lambda e, tb=tb, sub=sub, CFM=CFM: e.tensor_copy(
                CFM[:, 128 * sub:128 * (sub + 1)], tb[0:8, 0:128]), "v", 1)
            cf_toks.append(t_m)
            pp.release(pi, t_m)
        scrW, rtokW = b.scr["L2_ffn_We"]
        for ex in range(8):
            xns_tok = b.load_xns(xn_scr, cols, xns_use)
            xns_use = []
            mb, imb, freemb = pp.acquire()
            p.wait("tensor", freemb)
            p.waits("tensor", cf_toks)
            p.wait("tensor", b.tk_oh8)
            t_mb = p.emit("tensor", lambda e, mb=mb, ex=ex, CFM=CFM: e.matmul(
                mb.ap()[:, :], sb["OH8"][:, 128 * ex:128 * (ex + 1)], CFM[:, :],
                start=True, stop=True), "t", 1)
            xi_toks = []
            for c in range(NCD):
                p.wait("vector", t_mb)
                p.wait("vector", xns_tok)
                t_xi = p.emit("vector", lambda e, c=c, mb=mb: e.tensor_tensor(
                    sb["SCR9"][:, c, :], sb["SCR9"][:, c, :], mb.ap()[:, :], ALU.mult), "v", 1)
                xi_toks.append(t_xi)
            pp.release(imb, xi_toks[-1])
            xns_use += xi_toks
            half = ex % 2
            p.wait("sync", rtokW)
            p.wait("sync", slot_busy[half])
            WeV = sb["AR3"][:, half]
            tk_we = p.dma("sync", f"l2w{half}", lambda e, ex=ex, WeV=WeV, scrW=scrW: e.dma_start(
                out=WeV, in_=scrW[ex].rearrange("(c p) o -> p c o", p=128)))
            t_l = None
            for oc in range(NCD):
                bank, ibk, free = pp.acquire()
                p.wait("tensor", free)
                p.wait("tensor", tk_we)
                for c in range(NCD):
                    p.wait("tensor", xi_toks[c])
                    t_l = p.emit("tensor", lambda e, bank=bank, c=c, oc=oc, WeV=WeV: e.matmul(
                        bank.ap()[:, :], WeV[:, c, 128 * oc:128 * (oc + 1)],
                        sb["SCR9"][:, c, :], start=(c == 0), stop=(c == NCD - 1)), "t", 1)
                xns_use.append(t_l)
                p.wait("vector", t_l)
                p.wait("vector", t_z)
                t_acc = p.emit("vector", lambda e, bank=bank, oc=oc, OUT_G=OUT_G: e.tensor_tensor(
                    OUT_G[:, 512 * oc:512 * (oc + 1)], bank.ap()[:, :],
                    OUT_G[:, 512 * oc:512 * (oc + 1)], ALU.add), "v", 1)
                pp.release(ibk, t_acc)
            slot_busy[half] = t_l
        p.wait("vector", ("v", p.cnt["v"]))
        for oc in range(NCD):
            p.waits("vector", out_evac_toks)
            t_x = p.emit("vector", lambda e, oc=oc, cols=cols, OUT_G=OUT_G: e.tensor_tensor(
                sb["X"][:, oc, cols], OUT_G[:, 512 * oc:512 * (oc + 1)],
                sb["X"][:, oc, cols], ALU.add), "v", 1)
            out_evac_toks.append(t_x)
    b.x_toks = out_evac_toks



def l3sel_for_core(m):
    sel = np.zeros((10, 16), dtype=np.float32)
    for k in range(m):
        sel[k, 2 * (m - k) - 2] = 1.0
    sel[8, 2 * m] = 1.0
    sel[9, 2 * m + 1] = 1.0
    return sel


def block3(b: Build, n_cc):
    p, sb, pp, nc = b.p, b.sb, b.pp, b.nc
    R0 = sb["ROW"]
    AR6F = sb["AR6"].ap().rearrange("p c -> p c").bitcast(F32)
    F1 = sb["AR1"].ap().rearrange("p c t -> p (c t)")
    F1F = F1.bitcast(F32)
    F2 = sb["AR2"].ap().rearrange("p a b -> p (a b)")
    F3 = sb["AR3"]
    f5 = b.f5
    F5F = f5.bitcast(F32)

    LNW, LNB = 0, 8
    CRE, CIM, CIMN = 16, 24, 32
    TGA, TT1, TT2, TT3, TT4 = 40, 48, 56, 896, 904
    TBL = 64
    CORE_, COIM, COIMN = 320, 384, 448
    CP0, CP1, LC0, LC1, LAST = 512, 520, 528, 536, 544
    SELB = 560
    S1C, S2C, SH1, SH2, S3C = 720, 752, 784, 816, 848
    B1F, B2F = 856, 888


    xn_scr = nc.dram_tensor("xn_scr3", [NCD, 128, T], BF16).ap()
    h_scr = nc.dram_tensor("h_scr3", [NCD, 128, T], BF16).ap()
    xa_scr = nc.dram_tensor("xa_scr3", [NCD, 128, T], BF16).ap()
    cc_in = nc.dram_tensor("cc_in", [2, NCD, 128, S], BF16).ap()
    cc_out = nc.dram_tensor("cc_out", [n_cc, 2, NCD, 128, S], BF16).ap()

    p.wait("sync", ("t", p.cnt["t"]))
    p.wait("sync", ("v", p.cnt["v"]))
    p.wait("sync", ("s", p.cnt["s"]))
    ctoks = []
    def cload(dst_col, n, name, nchunk):
        return p.dma("sync", "b3c", lambda e, dst_col=dst_col, n=n, name=name, nchunk=nchunk: e.dma_start(
            out=AR6F[:, dst_col:dst_col + nchunk],
            in_=b.dr[name].rearrange("(c p) -> p c", p=128),
            allow_slow_non_contiguous=True))
    ctoks.append(cload(LNW, 1024, "L3_attn_ln_w", 8))
    ctoks.append(cload(LNB, 1024, "L3_attn_ln_b", 8))
    ctoks.append(cload(LAST, 1024, "L3_attn_last_conv_init", 8))
    ctoks.append(cload(S1C, 4096, "L3_ffn_w1_s", 32))
    ctoks.append(cload(S2C, 4096, "L3_ffn_w2_s", 32))
    ctoks.append(cload(SH1, 4096, "L3_ffn_h1_s", 32))
    ctoks.append(cload(SH2, 4096, "L3_ffn_h2_s", 32))
    ctoks.append(cload(S3C, 1024, "L3_ffn_w3_s", 8))
    ctoks.append(cload(B1F, 4096, "L3_attn_b1", 32))
    ctoks.append(cload(B2F, 1024, "L3_attn_b2", 8))
    tk_lm = cload(TT3, 1024, "L3_attn_lmlg", 8)
    tk_th = cload(TT4, 1024, "L3_attn_theta", 8)
    tk_sel = p.dma("sync", "b3c", lambda e: e.dma_start(
        out=R0[0:1, 1600:1760],
        in_=b.dr["l3sel"].rearrange("a b -> (a b)").rearrange("(a e) -> a e", a=1)))

    b.ln_xn(xn_scr)

    def vop(fn, waits=()):
        for w in waits:
            p.wait("vector", w)
        return p.emit("vector", fn, "v", 1)

    def sop(fn, waits=()):
        for w in waits:
            p.wait("scalar", w)
        return p.emit("scalar", fn, "s", 1)

    c6 = lambda c0, n=8: AR6F[:, c0:c0 + n]
    t_t = sop(lambda e: e.activation(c6(TGA), c6(TT3), AF.Exp), [tk_lm])
    t_gam = sop(lambda e: e.activation(c6(TT3), c6(TGA), AF.Exp, scale=-1.0), [t_t])
    t_s2 = sop(lambda e: e.activation(c6(TT1), c6(TT4), AF.Sin, scale=0.5), [tk_th])
    t_s4 = sop(lambda e: e.activation(c6(TT2), c6(TT4), AF.Sin, scale=0.25), [t_s2])
    t_q2 = sop(lambda e: e.activation(c6(TGA), c6(TT1), AF.Square), [t_s4, t_gam])
    t_cos = vop(lambda e: e.tensor_scalar(c6(TGA), c6(TGA), -2.0, 1.0, ALU.mult, ALU.add), [t_q2])
    t_q4 = sop(lambda e: e.activation(c6(TT4), c6(TT2), AF.Square), [t_q2])
    t_c2p = vop(lambda e: e.tensor_scalar(c6(TT4), c6(TT4), -4.0, 2.0, ALU.mult, ALU.add), [t_q4, t_cos])
    t_sin = vop(lambda e: e.tensor_tensor(c6(TT2), c6(TT1), c6(TT4), ALU.mult), [t_c2p])
    t_cre = vop(lambda e: e.tensor_tensor(c6(CRE), c6(TT3), c6(TGA), ALU.mult), [t_sin])
    t_cim = vop(lambda e: e.tensor_tensor(c6(CIM), c6(TT3), c6(TT2), ALU.mult), [t_cre])
    t_cimn = vop(lambda e: e.tensor_scalar(c6(CIMN), c6(CIM), -1.0, None, ALU.mult), [t_cim])
    tbre = lambda pw: AR6F[:, TBL + 16 * (pw - 1):TBL + 16 * (pw - 1) + 8]
    tbim = lambda pw: AR6F[:, TBL + 16 * (pw - 1) + 8:TBL + 16 * (pw - 1) + 16]
    t_a = vop(lambda e: e.tensor_copy(tbre(1), c6(CRE)), [t_cimn])
    t_b = vop(lambda e: e.tensor_copy(tbim(1), c6(CIM)), [t_a])
    prev = t_b
    for pw in range(2, 17):
        t1 = vop(lambda e, pw=pw: e.tensor_tensor(c6(TT1), tbre(pw - 1), c6(CRE), ALU.mult), [prev])
        t2 = vop(lambda e, pw=pw: e.tensor_tensor(c6(TT2), tbim(pw - 1), c6(CIM), ALU.mult), [t1])
        t3 = vop(lambda e, pw=pw: e.tensor_tensor(tbre(pw), c6(TT1), c6(TT2), ALU.subtract), [t2])
        t4 = vop(lambda e, pw=pw: e.tensor_tensor(c6(TT1), tbre(pw - 1), c6(CIM), ALU.mult), [t3])
        t5 = vop(lambda e, pw=pw: e.tensor_tensor(c6(TT2), tbim(pw - 1), c6(CRE), ALU.mult), [t4])
        prev = vop(lambda e, pw=pw: e.tensor_tensor(tbim(pw), c6(TT1), c6(TT2), ALU.add), [t5])
    p.wait("tensor", tk_sel)
    bsl, isl, t_bs = b.bcast_row(R0[0:1, 1600:1760], ncols=160)
    t_selb = vop(lambda e, bsl=bsl: e.tensor_copy(
        AR6F[:, SELB:SELB + 160], bsl.ap()[:, 0:160]), [t_bs, prev])
    pp.release(isl, t_selb)
    co_toks = []
    for k in range(8):
        tr = vop(lambda e, k=k: e.tensor_scalar(
            c6(CORE_ + 8 * k), tbre(1), AR6F[:, SELB + 16 * k:SELB + 16 * k + 1],
            None, ALU.mult), [t_selb])
        ti = vop(lambda e, k=k: e.tensor_scalar(
            c6(COIM + 8 * k), tbim(1), AR6F[:, SELB + 16 * k:SELB + 16 * k + 1],
            None, ALU.mult), [tr])
        for pw in range(2, 17):
            tr = vop(lambda e, k=k, pw=pw: e.scalar_tensor_tensor(
                c6(CORE_ + 8 * k), tbre(pw), AR6F[:, SELB + 16 * k + pw - 1:SELB + 16 * k + pw],
                c6(CORE_ + 8 * k), ALU.mult, ALU.add), [ti])
            ti = vop(lambda e, k=k, pw=pw: e.scalar_tensor_tensor(
                c6(COIM + 8 * k), tbim(pw), AR6F[:, SELB + 16 * k + pw - 1:SELB + 16 * k + pw],
                c6(COIM + 8 * k), ALU.mult, ALU.add), [tr])
        co_toks.append(ti)
    t_coimn = vop(lambda e: e.tensor_scalar(
        AR6F[:, COIMN:COIMN + 64], AR6F[:, COIM:COIM + 64], -1.0, None, ALU.mult), co_toks)
    COH = sb["AR6"].ap()[:, 1824:2016]
    COLO = 1008
    t_h_ = vop(lambda e: e.tensor_copy(COH, AR6F[:, CORE_:CORE_ + 192]), [t_coimn])
    t_coimn = vop(lambda e: e.tensor_tensor(
        AR6F[:, COLO:COLO + 192], AR6F[:, CORE_:CORE_ + 192], COH, ALU.subtract), [t_h_])
    tcp = {}
    for j, dst in ((0, CP0), (1, CP1)):
        tr = vop(lambda e, j=j, dst=dst: e.tensor_scalar(
            c6(dst), tbre(1), AR6F[:, SELB + 128 + 16 * j:SELB + 129 + 16 * j],
            None, ALU.mult), [t_coimn])
        for pw in range(2, 17):
            tr = vop(lambda e, j=j, dst=dst, pw=pw: e.scalar_tensor_tensor(
                c6(dst), tbre(pw), AR6F[:, SELB + 128 + 16 * j + pw - 1:SELB + 128 + 16 * j + pw],
                c6(dst), ALU.mult, ALU.add), [tr])
        tcp[j] = tr
    t_lc0 = vop(lambda e: e.tensor_tensor(c6(LC0), c6(LAST), c6(CP0), ALU.mult),
                [tcp[0], tcp[1]] + ctoks)
    t_lc1 = vop(lambda e: e.tensor_tensor(c6(LC1), c6(LAST), c6(CP1), ALU.mult), [t_lc0])
    b._b3_coef = t_lc1

    lnst_prev = [None]

    def ln2_scr9(xns_tok, dst_fn, extra_dst_waits=()):
        SQS = b.SQS
        bankA, ia, freeA = pp.acquire()
        p.wait("tensor", freeA)
        p.wait("tensor", xns_tok)
        p.wait("tensor", lnst_prev[0])
        tokA = None
        for c in range(NCD):
            tokA = p.emit("tensor", lambda e, bankA=bankA, c=c: e.matmul(
                bankA.ap()[0:1, :], sb["ONEB"][:, :], sb["SCR9"][:, c, :],
                start=(c == 0), stop=(c == NCD - 1)), "t", 1)
        p.wait("scalar", xns_tok)
        p.wait("scalar", ("t", p.cnt["t"]))
        p.wait("scalar", ("v", p.cnt["v"]))
        sq_toks = []
        for c in range(NCD):
            sq_toks.append(p.emit("scalar", lambda e, c=c, SQS=SQS: e.activation(
                SQS[:, 512 * c:512 * (c + 1)], sb["SCR9"][:, c, :], AF.Square), "s", 1))
        bankB, ib, freeB = pp.acquire()
        p.wait("tensor", freeB)
        p.waits("tensor", sq_toks)
        tokB = None
        for c in range(NCD):
            tokB = p.emit("tensor", lambda e, bankB=bankB, c=c, SQS=SQS: e.matmul(
                bankB.ap()[0:1, :], sb["ONEB"][:, :], SQS[:, 512 * c:512 * (c + 1)],
                start=(c == 0), stop=(c == NCD - 1)), "t", 1)
        t_mu = vop(lambda e, bankA=bankA: e.tensor_scalar(
            R0[0:1, 0:512], bankA.ap()[0:1, :], 1.0 / D, None, ALU.mult), [tokA])
        t_m2 = vop(lambda e, bankB=bankB: e.tensor_scalar(
            R0[0:1, 1024:1536], bankB.ap()[0:1, :], 1.0 / D, None, ALU.mult), [tokB])
        pp.release(ia, t_mu)
        pp.release(ib, t_m2)
        t_mu2 = vop(lambda e: e.tensor_tensor(
            R0[0:1, 512:1024], R0[0:1, 0:512], R0[0:1, 0:512], ALU.mult), [t_mu])
        t_var = vop(lambda e: e.tensor_tensor(
            R0[0:1, 1024:1536], R0[0:1, 1024:1536], R0[0:1, 512:1024], ALU.subtract),
            [t_m2, t_mu2])
        t_sd = sop(lambda e: e.activation(
            R0[0:1, 512:1024], R0[0:1, 1024:1536], AF.Sqrt, bias=1e-5), [t_var])
        t_rq = vop(lambda e: e.reciprocal(
            R0[0:1, 1024:1536], R0[0:1, 512:1024]), [t_sd])
        p.wait("tensor", t_mu)
        bmu, imu, tmu = b.bcast_row(R0[0:1, 0:512])
        p.wait("tensor", t_rq)
        brq, irq, trq = b.bcast_row(R0[0:1, 1024:1536])
        toks = []
        prev_pair = [None, None]
        for c in range(NCD):
            tmp = b.TMP_A if c % 2 == 0 else b.TMP_B
            t1 = vop(lambda e, c=c, bmu=bmu, tmp=tmp: e.tensor_tensor(
                tmp, sb["SCR9"][:, c, :], bmu.ap()[:, :], ALU.subtract),
                [tmu, prev_pair[c % 2]])
            t2 = vop(lambda e, c=c, brq=brq, tmp=tmp: e.tensor_tensor(
                tmp, tmp, brq.ap()[:, :], ALU.mult), [trq, t1])
            for w in extra_dst_waits:
                p.wait("vector", w)
            t3 = vop(lambda e, c=c, tmp=tmp, d=dst_fn(c): e.tensor_scalar(
                d, tmp,
                AR6F[:, LNW + c:LNW + c + 1],
                AR6F[:, LNB + c:LNB + c + 1],
                ALU.mult, ALU.add), [t2] + list(ctoks[:2]))
            prev_pair[c % 2] = t3
            toks.append(t3)
        pp.release(imu, toks[-1])
        pp.release(irq, toks[-1])
        lnst_prev[0] = toks[-1]
        return toks

    h0c = lambda ch: F1[:, 512 * ch:512 * (ch + 1)]
    h1c = lambda ch: F1[:, 4096 + 512 * ch:4096 + 512 * (ch + 1)]
    arec = lambda ch: F1[:, 8192 + 512 * ch:8192 + 512 * (ch + 1)]
    aimc = lambda ch: F1[:, 12288 + 512 * ch:12288 + 512 * (ch + 1)]
    xns_use = []
    h_toks = []
    prev_a = []
    for gc in range(4):
        tcols = slice(512 * gc, 512 * (gc + 1))
        cols0 = slice(512 * gc, 512 * (gc + 1))
        cols1 = slice(2048 + 512 * gc, 2048 + 512 * (gc + 1))
        xt0 = b.load_xns(xn_scr, cols0, xns_use)
        xns_use = []
        tk0 = ln2_scr9(xt0, h0c, extra_dst_waits=prev_a + [
            p.stream_last.get("hw"), p.stream_last.get("cci")])
        xns_use += tk0
        for ch in range(NCD):
            p.wait("sync", tk0[ch])
            h_toks.append(p.dma("sync", "hw", lambda e, ch=ch, cols0=cols0: e.dma_start(
                out=h_scr[ch, :, cols0], in_=h0c(ch))))
        xt1 = b.load_xns(xn_scr, cols1, xns_use)
        xns_use = []
        tk1 = ln2_scr9(xt1, h1c, extra_dst_waits=prev_a + [
            p.stream_last.get("hw"), p.stream_last.get("cci")])
        xns_use += tk1
        for ch in range(NCD):
            p.wait("sync", tk1[ch])
            h_toks.append(p.dma("sync", "hw", lambda e, ch=ch, cols1=cols1: e.dma_start(
                out=h_scr[ch, :, cols1], in_=h1c(ch))))
        prev_a = []
        a_toks = []
        for ch in range(NCD):
            t_ar = vop(lambda e, ch=ch: e.scalar_tensor_tensor(
                arec(ch), h0c(ch), AR6F[:, CRE + ch:CRE + ch + 1], h1c(ch),
                ALU.mult, ALU.add), [b._b3_coef, tk0[ch], tk1[ch]])
            t_ai = vop(lambda e, ch=ch: e.tensor_scalar(
                aimc(ch), h0c(ch), AR6F[:, CIM + ch:CIM + ch + 1], None,
                ALU.mult), [t_ar])
            p.wait("sync", t_ar)
            p.wait("sync", t_ai)
            a_toks.append(p.dma("sync", "cci", lambda e, ch=ch, tcols=tcols: e.dma_start(
                out=cc_in[0, ch, :, tcols], in_=arec(ch))))
            a_toks.append(p.dma("sync", "cci", lambda e, ch=ch, tcols=tcols: e.dma_start(
                out=cc_in[1, ch, :, tcols], in_=aimc(ch))))
            prev_a += [t_ar, t_ai]
        xns_use += [t_ar for t_ar in prev_a]
    b._h_toks = h_toks

    p.wait("gpsimd", p.stream_last.get("cci"))
    cc_in_f = cc_in.rearrange("a b c d -> (a b c d)")
    cc_out_f = cc_out.rearrange("k a b c d -> (k a b c d)")
    t_cc = p.emit("gpsimd", lambda e: e.collective_compute(
        "AllGather", ALU.bypass, replica_groups=[list(range(n_cc))],
        ins=[cc_in_f.opt()], outs=[cc_out_f.opt()]), "c", 1)
    b._cc_tok = t_cc

    l3_ffn(b, xn_scr)

    l3_combine(b, n_cc, cc_out, h_scr, xn_scr, xa_scr,
               AR6F, CORE_, COIM, COIMN, CRE, CIMN, LC0, LC1)

    l3_mlp(b, xa_scr, AR6F, LNW, LNB, B1F, B2F, ln2_scr9)


def l3_ffn(b: Build, xn_scr):
    p, sb, pp = b.p, b.sb, b.pp
    AR6F = sb["AR6"].ap().rearrange("p c -> p c").bitcast(F32)
    S1C, S2C, SH1, SH2, S3C = 720, 752, 784, 816, 848
    F1 = sb["AR1"].ap().rearrange("p c t -> p (c t)")
    F2 = sb["AR2"].ap().rearrange("p a b -> p (a b)")
    F5F = b.f5.bitcast(F32)
    hA = lambda hc: F1[:, 512 * hc:512 * (hc + 1)]
    hB = lambda hc: F2[:, 512 * hc:512 * (hc + 1)]
    scr1, rt1 = b.scr["L3_ffn_w1_w"]
    scr2, rt2 = b.scr["L3_ffn_w2_w"]
    scrh1, rth1 = b.scr["L3_ffn_h1_w"]
    scrh2, rth2 = b.scr["L3_ffn_h2_w"]
    scr3, rt3 = b.scr["L3_ffn_w3_w"]
    slot_busy = [None, None]
    xns_use = []
    x_toks = list(b.x_toks)
    p.wait("vector", p.stream_last.get("hw"))
    p.wait("vector", p.stream_last.get("cci"))
    p.wait("sync", ("t", p.cnt["t"]))
    for gc in range(NCD):
        cols = slice(512 * gc, 512 * (gc + 1))
        xt = b.load_xns(xn_scr, cols, xns_use)
        xns_use = []
        ev1 = []
        for pc in range(4):
            tkw = []
            for half, (scr, rt) in ((0, (scr1, rt1)), (1, (scr2, rt2))):
                p.wait("sync", rt)
                p.wait("sync", slot_busy[half])
                tkw.append(p.dma("sync", f"l3w{half}", lambda e, pc=pc, half=half, scr=scr: e.dma_start(
                    out=mkap(sb["AR3"], 8192 * half, [[1024, 8], [1, 1024]]),
                    in_=scr[:, 1024 * pc:1024 * (pc + 1)].rearrange("(c p) o -> p c o", p=128))))
            t_l = None
            for sub in range(8):
                hc = 8 * pc + sub
                bA, iA, frA = pp.acquire()
                bB, iB, frB = pp.acquire()
                p.wait("tensor", frA)
                p.wait("tensor", frB)
                p.wait("tensor", tkw[0])
                p.wait("tensor", tkw[1])
                p.wait("tensor", xt)
                tmA = tmB = None
                for c in range(NCD):
                    tmA = p.emit("tensor", lambda e, bA=bA, c=c, sub=sub: e.matmul(
                        bA.ap()[:, :], mkap(sb["AR3"], 1024 * c + 128 * sub, [[1, 128]]),
                        sb["SCR9"][:, c, :], start=(c == 0), stop=(c == NCD - 1)), "t", 1)
                for c in range(NCD):
                    tmB = p.emit("tensor", lambda e, bB=bB, c=c, sub=sub: e.matmul(
                        bB.ap()[:, :], mkap(sb["AR3"], 8192 + 1024 * c + 128 * sub, [[1, 128]]),
                        sb["SCR9"][:, c, :], start=(c == 0), stop=(c == NCD - 1)), "t", 1)
                t_l = tmB
                tmp = F5F[:, 512 * (sub % 2):512 * (sub % 2 + 1)]
                p.wait("scalar", tmA)
                p.wait("scalar", ("v", p.cnt["v"]))
                t_e1 = p.emit("scalar", lambda e, bA=bA, hc=hc, tmp=tmp: e.activation(
                    tmp, bA.ap()[:, :], AF.Silu,
                    scale=AR6F[:, S1C + hc:S1C + hc + 1]), "s", 1)
                pp.release(iA, t_e1)
                p.wait("vector", t_e1)
                p.wait("vector", tmB)
                p.wait("vector", ("t", p.cnt["t"]) if gc > 0 and hc == 0 else None)
                t_h = p.emit("vector", lambda e, bB=bB, hc=hc, tmp=tmp: e.scalar_tensor_tensor(
                    hA(hc), bB.ap()[:, :], AR6F[:, S2C + hc:S2C + hc + 1], tmp,
                    ALU.mult, ALU.mult), "v", 1)
                pp.release(iB, t_h)
                ev1.append(t_h)
                xns_use += [tmA, tmB]
            slot_busy[0] = tmA
            slot_busy[1] = tmB
        for stage, (scrh, rth, sc0, src, dst) in enumerate((
                (scrh1, rth1, SH1, hA, hB), (scrh2, rth2, SH2, hB, hA))):
            ev2 = []
            for pc in range(16):
                half = pc % 2
                p.wait("sync", rth)
                p.wait("sync", slot_busy[half])
                p.wait("sync", ("v", p.cnt["v"]) if pc < 2 else None)
                tkw = p.dma("sync", f"l3w{half}", lambda e, pc=pc, half=half, scrh=scrh: e.dma_start(
                    out=mkap(sb["AR3"], 8192 * half, [[256, 32], [1, 256]]),
                    in_=scrh[:, 256 * pc:256 * (pc + 1)].rearrange("(c p) o -> p c o", p=128)))
                t_l = None
                for j2 in range(2):
                    oc = 2 * pc + j2
                    bk, ik, frk = pp.acquire()
                    p.wait("tensor", frk)
                    p.wait("tensor", tkw)
                    if pc == 0 and j2 == 0:
                        p.waits("tensor", ev1 if stage == 0 else ev2_prev)
                    for ic in range(32):
                        t_l = p.emit("tensor", lambda e, bk=bk, ic=ic, j2=j2, half=half, src=src: e.matmul(
                            bk.ap()[:, :], mkap(sb["AR3"], 8192 * half + 256 * ic + 128 * j2, [[1, 128]]),
                            src(ic), start=(ic == 0), stop=(ic == 31)), "t", 1)
                    p.wait("scalar", t_l)
                    p.wait("scalar", ("v", p.cnt["v"]) if stage == 1 and pc == 0 else None)
                    t_g = p.emit("scalar", lambda e, bk=bk, oc=oc, sc0=sc0, dst=dst: e.activation(
                        dst(oc), bk.ap()[:, :], AF.Gelu,
                        scale=AR6F[:, sc0 + oc:sc0 + oc + 1]), "s", 1)
                    pp.release(ik, t_g)
                    ev2.append(t_g)
                slot_busy[half] = t_l
            ev2_prev = ev2
        for pc in range(4):
            half = pc % 2
            p.wait("sync", rt3)
            p.wait("sync", slot_busy[half])
            tkw = p.dma("sync", f"l3w{half}", lambda e, pc=pc, half=half, scr3=scr3: e.dma_start(
                out=mkap(sb["AR3"], 8192 * half, [[256, 32], [1, 256]]),
                in_=scr3[:, 256 * pc:256 * (pc + 1)].rearrange("(c p) o -> p c o", p=128)))
            t_l = None
            for j2 in range(2):
                oc = 2 * pc + j2
                bk, ik, frk = pp.acquire()
                p.wait("tensor", frk)
                p.wait("tensor", tkw)
                if pc == 0 and j2 == 0:
                    p.waits("tensor", ev2_prev)
                for ic in range(32):
                    t_l = p.emit("tensor", lambda e, bk=bk, ic=ic, j2=j2, half=half: e.matmul(
                        bk.ap()[:, :], mkap(sb["AR3"], 8192 * half + 256 * ic + 128 * j2, [[1, 128]]),
                        hA(ic), start=(ic == 0), stop=(ic == 31)), "t", 1)
                p.wait("vector", t_l)
                p.waits("vector", x_toks)
                t_x = p.emit("vector", lambda e, bk=bk, oc=oc, cols=cols: e.scalar_tensor_tensor(
                    sb["X"][:, oc, cols], bk.ap()[:, :], AR6F[:, S3C + oc:S3C + oc + 1],
                    sb["X"][:, oc, cols], ALU.mult, ALU.add), "v", 1)
                pp.release(ik, t_x)
                if pc == 0 and j2 == 0:
                    x_toks = []
                x_toks.append(t_x)
            slot_busy[half] = t_l
    b.x_toks = x_toks
    b._l3_slot_busy = slot_busy


def l3_combine(b: Build, n_cc, cc_out, h_scr, xn_scr, xa_scr,
               AR6F, CORE_, COIM, COIMN, CRE, CIMN, LC0, LC1):
    p, sb, pp = b.p, b.sb, b.pp
    F2 = sb["AR2"].ap().rearrange("p a b -> p (a b)")
    f5 = b.f5
    f9 = b.f9
    R0 = sb["ROW"]
    ROWB = R0.ap().bitcast(BF16)
    COLO = 1008
    p.wait("vector", ("t", p.cnt["t"]))
    p.wait("vector", ("s", p.cnt["s"]))
    p.wait("sync", ("t", p.cnt["t"]))
    p.wait("sync", ("s", p.cnt["s"]))
    p.wait("sync", ("v", p.cnt["v"]))
    p.wait("scalar", ("t", p.cnt["t"]))
    p.wait("scalar", ("v", p.cnt["v"]))
    p.waits("sync", b.xn_toks)
    p.waits("sync", b._h_toks)
    p.waits("scalar", b.xn_toks)
    p.waits("scalar", b._h_toks)
    xa_toks = []
    x_toks = list(b.x_toks)
    dg = lambda j: f5[:, 128 * j:128 * (j + 1)]
    slc_busy = [None] * 8
    ld_busy = [None, None]
    dg_prev = [None] * 48
    dcols = lambda k, ch: (
        (CORE_ + 8 * k + ch, 0), (COLO + 8 * k + ch, 0),
        (COIMN + 8 * k + ch, 2), (COLO + 128 + 8 * k + ch, 2),
        (COIM + 8 * k + ch, 4), (COLO + 64 + 8 * k + ch, 4))
    for ch in range(NCD):
        dg_toks = []
        for k in range(n_cc):
            for jj, (col, base) in enumerate(dcols(k, ch)):
                j = 6 * k + (base + (jj % 2))
                p.wait("vector", b._b3_coef)
                p.wait("vector", dg_prev[j])
                t_d = p.emit("vector", lambda e, j=j, col=col: e.tensor_scalar(
                    dg(j), sb["IDB"][:, :],
                    AR6F[:, col:col + 1], None, ALU.mult), "v", 1)
                dg_toks.append(t_d)
                dg_prev[j] = None
        for gc in range(4):
            tcols = slice(512 * gc, 512 * (gc + 1))
            cols0 = slice(512 * gc, 512 * (gc + 1))
            cols1 = slice(2048 + 512 * gc, 2048 + 512 * (gc + 1))
            a_toks = [None] * 16

            def a_load(k, tcols=tcols, ch=ch):
                sre, sim_ = (2 * k) % 8, (2 * k + 1) % 8
                p.wait("sync", slc_busy[sre])
                p.wait("sync", b._cc_tok)
                a_toks[2 * k] = p.dma("sync", "ccl", lambda e, k=k, ch=ch, sre=sre, tcols=tcols: e.dma_start(
                    out=f9[:, 512 * sre:512 * (sre + 1)],
                    in_=cc_out[k, 0, ch, :, tcols]))
                p.wait("scalar", slc_busy[sim_])
                p.wait("scalar", b._cc_tok)
                a_toks[2 * k + 1] = p.dma("scalar", "ccs", lambda e, k=k, ch=ch, sim_=sim_, tcols=tcols: e.dma_start(
                    out=f9[:, 512 * sim_:512 * (sim_ + 1)],
                    in_=cc_out[k, 1, ch, :, tcols]))

            for k in range(min(n_cc, 4)):
                a_load(k)
            hb = 2048 * (gc % 2)
            h0s = F2[:, hb:hb + 512]
            h1s = F2[:, hb + 512:hb + 1024]
            xn0s = F2[:, hb + 1024:hb + 1536]
            xn1s = F2[:, hb + 1536:hb + 2048]
            p.wait("sync", ld_busy[gc % 2])
            ld_toks = []
            for dst, scr, cc in ((h0s, h_scr, cols0), (h1s, h_scr, cols1),
                                 (xn0s, xn_scr, cols0), (xn1s, xn_scr, cols1)):
                ld_toks.append(p.dma("sync", "hld", lambda e, ch=ch, dst=dst, scr=scr, cc=cc: e.dma_start(
                    out=dst, in_=scr[ch, :, cc])))
            bre, ire, fre = pp.acquire()
            bim, iim, fim = pp.acquire()
            p.wait("tensor", fre)
            p.wait("tensor", fim)
            p.waits("tensor", dg_toks)
            t_mre = t_mim = None
            for k in range(n_cc):
                sre, sim_ = (2 * k) % 8, (2 * k + 1) % 8
                p.wait("tensor", a_toks[2 * k])
                p.wait("tensor", a_toks[2 * k + 1])
                for j, slot, last in ((6 * k + 0, sre, False), (6 * k + 1, sre, False),
                                      (6 * k + 2, sim_, False),
                                      (6 * k + 3, sim_, k == n_cc - 1)):
                    t_mre = p.emit("tensor", lambda e, bre=bre, j=j, slot=slot, k=k, last=last: e.matmul(
                        bre.ap()[:, :], dg(j), f9[:, 512 * slot:512 * (slot + 1)],
                        start=(k == 0 and j == 0), stop=last), "t", 1)
                for j, slot, last in ((6 * k + 0, sim_, False), (6 * k + 1, sim_, False),
                                      (6 * k + 4, sre, False),
                                      (6 * k + 5, sre, k == n_cc - 1)):
                    t_mim = p.emit("tensor", lambda e, bim=bim, j=j, slot=slot, k=k, last=last: e.matmul(
                        bim.ap()[:, :], dg(j), f9[:, 512 * slot:512 * (slot + 1)],
                        start=(k == 0 and j == 6 * k), stop=last), "t", 1)
                slc_busy[sre] = t_mim
                slc_busy[sim_] = t_mim
                if k + 4 < n_cc:
                    a_load(k + 4)
            for j in range(6 * n_cc):
                dg_prev[j] = t_mim
            xa0 = ROWB[:, 2048 + 512 * (ch % 2):2560 + 512 * (ch % 2)]
            xa1 = ROWB[:, 3072 + 512 * (ch % 2):3584 + 512 * (ch % 2)]
            tmpa = R0[:, 512 * (ch % 2):512 * (ch % 2) + 512]
            p.waits("vector", ld_toks)
            p.wait("vector", t_mre)
            p.wait("vector", t_mim)
            p.waits("vector", [t for t in xa_toks[-4:]])
            t_r0 = p.emit("vector", lambda e, ch=ch, bre=bre, tmpa=tmpa, h0s=h0s: e.scalar_tensor_tensor(
                tmpa, bre.ap()[:, :], AR6F[:, LC0 + ch:LC0 + ch + 1], h0s,
                ALU.add, ALU.add), "v", 1)
            p.wait("vector", t_r0)
            t_o0 = p.emit("vector", lambda e, tmpa=tmpa, xa0=xa0, h0s=h0s: e.tensor_tensor(
                xa0, tmpa, h0s, ALU.mult), "v", 1)
            p.wait("vector", t_o0)
            t_xa0 = p.emit("vector", lambda e, xa0=xa0, xn0s=xn0s: e.tensor_tensor(
                xa0, xa0, xn0s, ALU.add), "v", 1)
            p.wait("vector", t_xa0)
            p.waits("vector", x_toks)
            t_x0 = p.emit("vector", lambda e, ch=ch, cols0=cols0, xa0=xa0: e.tensor_tensor(
                sb["X"][:, ch, cols0], sb["X"][:, ch, cols0], xa0, ALU.add), "v", 1)
            if gc == 0 and ch == 0:
                x_toks = []
            x_toks.append(t_x0)
            p.wait("sync", t_xa0)
            xa_toks.append(p.dma("sync", "xaw", lambda e, ch=ch, cols0=cols0, xa0=xa0: e.dma_start(
                out=xa_scr[ch, :, cols0], in_=xa0)))
            t_r1 = p.emit("vector", lambda e, ch=ch, bre=bre, tmpa=tmpa, h1s=h1s: e.scalar_tensor_tensor(
                tmpa, bre.ap()[:, :], AR6F[:, CRE + ch:CRE + ch + 1], h1s,
                ALU.mult, ALU.add), "v", 1)
            p.wait("vector", t_r1)
            pp.release(ire, t_r1)
            t_r1b = p.emit("vector", lambda e, ch=ch, bim=bim, tmpa=tmpa: e.scalar_tensor_tensor(
                tmpa, bim.ap()[:, :], AR6F[:, CIMN + ch:CIMN + ch + 1], tmpa,
                ALU.mult, ALU.add), "v", 1)
            p.wait("vector", t_r1b)
            pp.release(iim, t_r1b)
            t_r1c = p.emit("vector", lambda e, ch=ch, tmpa=tmpa, h0s=h0s: e.scalar_tensor_tensor(
                tmpa, h0s, AR6F[:, CRE + ch:CRE + ch + 1], tmpa,
                ALU.mult, ALU.add), "v", 1)
            p.wait("vector", t_r1c)
            t_r1d = p.emit("vector", lambda e, ch=ch, tmpa=tmpa: e.tensor_scalar(
                tmpa, tmpa, AR6F[:, LC1 + ch:LC1 + ch + 1], None, ALU.add), "v", 1)
            p.wait("vector", t_r1d)
            t_o1 = p.emit("vector", lambda e, tmpa=tmpa, xa1=xa1, h1s=h1s: e.tensor_tensor(
                xa1, tmpa, h1s, ALU.mult), "v", 1)
            p.wait("vector", t_o1)
            t_xa1 = p.emit("vector", lambda e, xa1=xa1, xn1s=xn1s: e.tensor_tensor(
                xa1, xa1, xn1s, ALU.add), "v", 1)
            p.wait("vector", t_xa1)
            t_x1 = p.emit("vector", lambda e, ch=ch, cols1=cols1, xa1=xa1: e.tensor_tensor(
                sb["X"][:, ch, cols1], sb["X"][:, ch, cols1], xa1, ALU.add), "v", 1)
            x_toks.append(t_x1)
            p.wait("sync", t_xa1)
            xa_toks.append(p.dma("sync", "xaw", lambda e, ch=ch, cols1=cols1, xa1=xa1: e.dma_start(
                out=xa_scr[ch, :, cols1], in_=xa1)))
            ld_busy[gc % 2] = t_xa1
    b.x_toks = x_toks
    b._xa_toks = xa_toks


def l3_mlp(b: Build, xa_scr, AR6F, LNW, LNB, B1F, B2F, ln2_scr9):
    p, sb, pp = b.p, b.sb, b.pp
    F1 = sb["AR1"].ap().rearrange("p c t -> p (c t)")
    hA = lambda hc: F1[:, 512 * hc:512 * (hc + 1)]
    scr1, rt1 = b.scr["L3_attn_w1"]
    scr2, rt2 = b.scr["L3_attn_w2"]
    slot_busy = b._l3_slot_busy
    xns_use = []
    x_toks = list(b.x_toks)
    first = True
    for gc in range(NCD):
        cols = slice(512 * gc, 512 * (gc + 1))
        xt = b.load_xns(xa_scr, cols, xns_use, src_toks=b._xa_toks)
        xns_use = []
        tkn = ln2_scr9(xt, lambda ch: sb["SCR9"][:, ch, :])
        ev1 = []
        for pc in range(4):
            half = pc % 2
            p.wait("sync", rt1)
            p.wait("sync", slot_busy[half])
            tkw = p.dma("sync", f"l3w{half}", lambda e, pc=pc, half=half, scr1=scr1: e.dma_start(
                out=mkap(sb["AR3"], 8192 * half, [[1024, 8], [1, 1024]]),
                in_=scr1[:, 1024 * pc:1024 * (pc + 1)].rearrange("(c p) o -> p c o", p=128)))
            t_l = None
            for sub in range(8):
                hc = 8 * pc + sub
                bk, ik, frk = pp.acquire()
                p.wait("tensor", frk)
                p.wait("tensor", tkw)
                p.waits("tensor", tkn)
                if first and hc == 0:
                    p.wait("tensor", ("v", p.cnt["v"]))
                    p.wait("tensor", ("p", p.cnt["p"]))
                for c in range(NCD):
                    t_l = p.emit("tensor", lambda e, bk=bk, c=c, sub=sub, half=half: e.matmul(
                        bk.ap()[:, :], mkap(sb["AR3"], 8192 * half + 1024 * c + 128 * sub, [[1, 128]]),
                        sb["SCR9"][:, c, :], start=(c == 0), stop=(c == NCD - 1)), "t", 1)
                xns_use.append(t_l)
                p.wait("scalar", t_l)
                if first and hc == 0:
                    p.wait("scalar", ("v", p.cnt["v"]))
                    p.wait("scalar", ("p", p.cnt["p"]))
                t_s = p.emit("scalar", lambda e, bk=bk, hc=hc: e.activation(
                    hA(hc), bk.ap()[:, :], AF.Silu,
                    bias=AR6F[:, B1F + hc:B1F + hc + 1]), "s", 1)
                pp.release(ik, t_s)
                ev1.append(t_s)
            slot_busy[half] = t_l
        first = False
        for pc in range(4):
            half = pc % 2
            p.wait("sync", rt2)
            p.wait("sync", slot_busy[half])
            tkw = p.dma("sync", f"l3w{half}", lambda e, pc=pc, half=half, scr2=scr2: e.dma_start(
                out=mkap(sb["AR3"], 8192 * half, [[256, 32], [1, 256]]),
                in_=scr2[:, 256 * pc:256 * (pc + 1)].rearrange("(c p) o -> p c o", p=128)))
            t_l = None
            for j2 in range(2):
                oc = 2 * pc + j2
                bk, ik, frk = pp.acquire()
                p.wait("tensor", frk)
                p.wait("tensor", tkw)
                if pc == 0 and j2 == 0:
                    p.waits("tensor", ev1)
                for ic in range(32):
                    t_l = p.emit("tensor", lambda e, bk=bk, ic=ic, j2=j2, half=half: e.matmul(
                        bk.ap()[:, :], mkap(sb["AR3"], 8192 * half + 256 * ic + 128 * j2, [[1, 128]]),
                        hA(ic), start=(ic == 0), stop=(ic == 31)), "t", 1)
                p.wait("vector", t_l)
                p.waits("vector", x_toks)
                t_x = p.emit("vector", lambda e, bk=bk, oc=oc, cols=cols: e.scalar_tensor_tensor(
                    sb["X"][:, oc, cols], bk.ap()[:, :], AR6F[:, B2F + oc:B2F + oc + 1],
                    sb["X"][:, oc, cols], ALU.add, ALU.add), "v", 1)
                pp.release(ik, t_x)
                if gc == 0 and pc == 0 and j2 == 0:
                    x_toks = []
                x_toks.append(t_x)
            slot_busy[half] = t_l
    b.x_toks = x_toks



def store_out(b: Build):
    p, sb, pp = b.p, b.sb, b.pp
    out_flat = b.dr["out"].rearrange("b s d -> (b s) d")
    OUTS = b.OUTS
    st_tok = [None] * 4
    for ti in range(T // 128):
        buf = ti % 4
        ost = OUTS[:, 1024 * buf:1024 * (buf + 1)]
        ev = []
        for c in range(NCD):
            tile, pi, free = pp.acquire()
            p.wait("tensor", free)
            p.waits("tensor", b.x_toks)
            tb = tile.ap().bitcast(BF16)
            t_tp = p.emit("tensor", lambda e, tb=tb, c=c, ti=ti: e.transpose(
                tb[:, 0:128], sb["X"][:, c, 128 * ti:128 * (ti + 1)], sb["IDB"][:, :]), "t", 1)
            p.wait("vector", t_tp)
            p.wait("vector", st_tok[buf])
            t_ev = p.emit("vector", lambda e, tb=tb, ost=ost, c=c: e.tensor_copy(
                ost[:, 128 * c:128 * (c + 1)], tb[:, 0:128]), "v", 1)
            ev.append(t_ev)
            pp.release(pi, t_ev)
        p.waits("sync", ev)
        t_st = p.dma("sync", f"st{buf}", lambda e, ost=ost, ti=ti: e.dma_start(
            out=out_flat[128 * ti:128 * (ti + 1), :], in_=ost))
        st_tok[buf] = t_st
    for bb in range(4):
        p.wait("sync", st_tok[bb])



WEIGHT_NAMES = [
    "L0_attn_Wq", "L0_attn_Wk", "L0_attn_Wv", "L0_attn_Wo",
    "L0_ffn_w_freq_real", "L0_ffn_w_freq_imag", "L0_ffn_bias",
]
if N_BLOCKS >= 2:
    WEIGHT_NAMES += [
        "L1_attn_wq", "L1_attn_wk", "L1_attn_wv", "L1_attn_wo",
        "L1_attn_q_gamma", "L1_attn_k_gamma",
        "L1_ffn_W1", "L1_ffn_b1", "L1_ffn_W2", "L1_ffn_b2", "L1_ffn_Wr", "L1_ffn_br",
    ]
if N_BLOCKS >= 3:
    WEIGHT_NAMES += [
        "L2_attn_Wq_amp", "L2_attn_Wk_amp", "L2_attn_Wq_phi", "L2_attn_Wk_phi",
        "L2_attn_Wv", "L2_attn_Wo", "L2_attn_score_log_scale",
        "L2_ffn_Wg", "L2_ffn_We",
    ]
if N_BLOCKS >= 4:
    WEIGHT_NAMES += [
        "L3_attn_lmlg", "L3_attn_theta", "L3_attn_last_conv_init",
        "L3_attn_ln_w", "L3_attn_ln_b",
        "L3_attn_w1", "L3_attn_b1", "L3_attn_w2", "L3_attn_b2",
        "L3_ffn_w1_w", "L3_ffn_w1_s", "L3_ffn_w2_w", "L3_ffn_w2_s",
        "L3_ffn_h1_w", "L3_ffn_h1_s", "L3_ffn_h2_w", "L3_ffn_h2_s",
        "L3_ffn_w3_w", "L3_ffn_w3_s",
    ]


def convert_all(b):
    for w in ("L0_attn_Wq", "L0_attn_Wk", "L0_attn_Wv", "L0_attn_Wo"):
        b.convert(w, transpose=True)
    b.convert("L0_ffn_w_freq_real", transpose=False)
    b.convert("L0_ffn_w_freq_imag", transpose=False)
    if N_BLOCKS >= 2:
        for w in ("L1_attn_wq", "L1_attn_wk", "L1_attn_wv", "L1_attn_wo"):
            b.convert(w, transpose=True)
        b.convert("L1_ffn_W1", transpose=False)
        b.convert("L1_ffn_W2", transpose=False)
    if N_BLOCKS >= 3:
        for w in ("L2_attn_Wq_amp", "L2_attn_Wk_amp", "L2_attn_Wq_phi",
                  "L2_attn_Wk_phi", "L2_attn_Wv", "L2_attn_Wo"):
            b.convert(w, transpose=True)
        b.convert("L2_ffn_We", transpose=False)
    if N_BLOCKS >= 4:
        for w in ("L3_ffn_w1_w", "L3_ffn_w2_w", "L3_ffn_h1_w", "L3_ffn_h2_w",
                  "L3_ffn_w3_w"):
            b.convert(w, transpose=True)
        for w in ("L3_attn_w1", "L3_attn_w2"):
            b.convert(w, transpose=False)


def build_all(const_shapes, weight_shapes, n_cc=N_CORES):
    b = Build(const_shapes, weight_shapes)
    if N_BLOCKS >= 4:
        b.dr["l3sel"] = b.nc.dram_tensor(
            "l3sel", [10, 16], F32, kind="ExternalInput").ap()
    b.register_float(1e-5)
    b.register_float(1e-6)
    b.register_float(1.0)
    b.load_consts()
    b.load_x()
    convert_all(b)
    block0(b)
    if N_BLOCKS >= 2:
        block1(b)
    if N_BLOCKS >= 3:
        block2(b)
    if N_BLOCKS >= 4:
        block3(b, n_cc)
    store_out(b)
    b.p.build()
    return b


def kernel(**inputs):
    consts = host_constants()
    const_shapes = {k: (v.shape, BF16 if v.dtype == BF else F32) for k, v in consts.items()}
    weight_shapes = {k: list(np.asarray(inputs[k]).shape) for k in WEIGHT_NAMES}
    b = build_all(const_shapes, weight_shapes)

    x = np.ascontiguousarray(inputs["L0_attn_x"])
    in_maps = []
    for c in range(N_CORES):
        m = {"x": x[BPC * c:BPC * (c + 1)]}
        for k, v in consts.items():
            m[k] = v
        for k in WEIGHT_NAMES:
            m[k] = np.ascontiguousarray(inputs[k], dtype=np.float32)
        if N_BLOCKS >= 4:
            m["l3sel"] = l3sel_for_core(c)
        in_maps.append(m)
    res = run_bass_kernel_spmd(b.nc, in_maps, core_ids=list(range(N_CORES)))
    out = np.concatenate([r["out"].reshape(BPC, S, D) for r in res.results], axis=0)
    b.stack.close()
    return out
